# version 17
# baseline (speedup 1.0000x reference)
import numpy as np
import ml_dtypes

import concourse.bass as bass
import concourse.mybir as mybir
from concourse.bass_utils import run_bass_kernel_spmd

F32 = mybir.dt.float32
BF16 = mybir.dt.bfloat16
I32 = mybir.dt.int32
AF = mybir.ActivationFunctionType
ALU = mybir.AluOpType

D = 4096
NCH = 32
TL = 1024
EXP_SHIFT = 10.0
EPS = 1e-5
INV_SQRT_HD = float(1.0 / np.sqrt(128.0))


def owned_tiles(cc):
    return sorted([2 * cc, 2 * cc + 1, 14 - 2 * cc, 15 - 2 * cc,
                   16 + 2 * cc, 17 + 2 * cc, 30 - 2 * cc, 31 - 2 * cc])

OWNED = {cc: owned_tiles(cc) for cc in range(4)}
GK = [OWNED[r][jj] for r in range(4) for jj in range(8)]
G1MAX = [7, 15, 23, 31]
G0MIN = [0, 8, 16, 24]
S_W = [[j for j in range(32) if GK[j] <= G1MAX[w]] for w in range(4)]
MASKED_W = [[j for j in S_W[w] if GK[j] >= G0MIN[w]] for w in range(4)]
MASK_SLOT = {}
for _w in range(4):
    for _k, _j in enumerate(MASKED_W[_w]):
        MASK_SLOT[(_w, _j)] = _k
N_MASKS_W = 8



class Prog:

    def __init__(self, nc):
        self.nc = nc
        self.ops = {e: [] for e in ("t", "v", "s", "g", "y")}
        self.cnt = {}
        self.phys = []
        self.pcnt = []
        self.map = {}
        self.pool_ptr = 0
        self.last_wait = {}
        self.gmap = {}
        self.gcnt = {}

    def new_scope(self):
        self.map = {}
        self.cnt = {}
        self.pool_ptr = 0

    def _entry(self, name):
        if name not in self.map:
            i = self.pool_ptr
            self.pool_ptr += 1
            if i >= len(self.phys):
                self.phys.append(self.nc.alloc_semaphore(f"S{i}"))
                self.pcnt.append(0)
            self.map[name] = (i, self.pcnt[i])
            self.cnt[name] = 0
        return self.map[name]

    def _gentry(self, name):
        if name not in self.gmap:
            self.gmap[name] = self.nc.alloc_semaphore(f"SG{len(self.gmap)}")
            self.gcnt[name] = 0
        return self.gmap[name]

    def sem(self, name):
        if name.startswith("g:"):
            return self._gentry(name)
        i, _ = self._entry(name)
        return self.phys[i]

    def op(self, eng, fn):
        self.ops[eng].append(fn)

    def opi(self, eng, fn, sname, by=1):
        if sname.startswith("g:"):
            s = self._gentry(sname)
            self.gcnt[sname] += by
            self.ops[eng].append(lambda E, fn=fn, s=s, by=by: fn(E).then_inc(s, by))
            return self.gcnt[sname]
        i, _ = self._entry(sname)
        self.cnt[sname] += by
        self.pcnt[i] += by
        s = self.phys[i]
        self.ops[eng].append(lambda E, fn=fn, s=s, by=by: fn(E).then_inc(s, by))
        return self.cnt[sname]

    def wait(self, eng, sname, val=None):
        if sname.startswith("g:"):
            s = self._gentry(sname)
            vv = self.gcnt[sname] if val is None else val
            if vv <= 0:
                return
            key = (eng, sname)
        else:
            i, carry = self._entry(sname)
            v = self.cnt[sname] if val is None else val
            if v <= 0:
                return
            s = self.phys[i]
            vv = carry + v
            key = (eng, i)
        if self.last_wait.get(key, -1) >= vv:
            return
        self.last_wait[key] = vv
        self.ops[eng].append(lambda E, s=s, v=vv: E.wait_ge(s, v))

    def finalize(self):
        with self.nc.Block() as block:
            def mk(ops):
                def run(E):
                    for fn in ops:
                        fn(E)
                return run
            block.tensor(mk(self.ops["t"]))
            block.vector(mk(self.ops["v"]))
            block.scalar(mk(self.ops["s"]))
            block.gpsimd(mk(self.ops["g"]))
            block.sync(mk(self.ops["y"]))


class Ctx:
    pass


def alloc_common(nc, P):
    c = Ctx()
    c.nc = nc
    c.P = P
    A = nc.alloc_sbuf_tensor
    c.xmir = A("xmir", [128, NCH * TL], BF16).ap()
    c.qb = A("qb", [128, 32 * TL], BF16).ap()
    c.wstrip = [A(f"wstrip{i}", [128, NCH * 128], BF16).ap() for i in range(2)]
    c.xstg = [A(f"xstg{i}", [128, 512], F32).ap() for i in range(2)]
    c.evac = [A(f"evac{i}", [128, 512], F32).ap() for i in range(2)]
    c.sq = A("sq", [128, TL], BF16).ap()
    c.rows = nc.alloc_sbuf_tensor_at("rows", [1, 2048], F32,
                                     offset=nc.lookup_mloc("qb").addr).ap()
    c.m_row = c.rows[0:1, 0:1024]
    c.r_row = c.rows[0:1, 1024:2048]
    c.mBb = A("mBb", [128, TL], BF16).ap()
    c.rBf = A("rBf", [128, TL], F32).ap()
    c.t_row = c.rBf[0:1, 0:1024]
    c.dn_row = c.rBf[0:1, 0:512]
    c.dn2_row = c.rBf[0:1, 512:1024]
    c.nrm = [A(f"nrm{i}", [128, 256], F32).ap() for i in range(2)]
    c.consts = A("consts", [128, 8], F32).ap()
    c.C_SCALE, c.C_SHIFT, c.C_EPS, c.C_INVD, c.C_NEGH = 0, 1, 2, 3, 4
    c.ones_b = A("ones_b", [128, 1], BF16).ap()
    c.ones_f = A("ones_f", [1, 128], F32).ap()
    c.ones_fc = A("ones_fc", [128, 1], F32).ap()
    c.ones_br = A("ones_br", [1, 128], BF16).ap()

    PA = nc.alloc_psum_tensor
    c.pp = [PA(f"pp{i}", [128, 512], F32).ap() for i in range(2)]
    c.pd = [PA(f"pd{i}", [128, 512], F32).ap() for i in range(2)]
    c.ssb = PA("ssb", [128, 2048], F32).ap()
    c.ss = [c.ssb[:, 0:512], c.ssb[:, 512:1024]]
    c.pt = [c.ssb[:, 1024:1536], c.ssb[:, 1536:2048]]

    c.nrm2 = A("nrm2", [128, 256], F32).ap()
    c.A = nc.alloc_sbuf_tensor

    def init(v):
        v.memset(c.consts[:, 0:1], INV_SQRT_HD)
        v.memset(c.consts[:, 1:2], -EXP_SHIFT)
        v.memset(c.consts[:, 2:3], EPS)
        v.memset(c.consts[:, 3:4], 1.0 / 4096.0)
        v.memset(c.consts[:, 4:5], -0.5)
        v.memset(c.ones_b[:], 1.0)
        v.memset(c.ones_f[:], 1.0)
        v.memset(c.ones_fc[:], 1.0)
        v.memset(c.pd[0][:, :], 0.0)
        v.memset(c.pd[1][:, :], 0.0)
        return v.memset(c.ones_br[:], 1.0)
    P.opi("v", init, "init", 1)
    for e in ("t", "s", "g", "y"):
        P.wait(e, "init", 1)
    return c



def emit_stats_ln(c, x_dram, tag, hooks=None, pre_sem=None):
    P = c.P
    sqs, pes = f"{tag}_sq", f"{tag}_pe"
    ld = [f"{tag}_ld0", f"{tag}_ld1"]
    s1b = [c.pt[0], c.ss[0]]
    s2b = [c.pt[1], c.ss[1]]
    for dc in range(NCH):
        if pre_sem is None:
            if dc >= 2:
                P.wait("g", sqs, dc - 1)
            P.opi("g", lambda E, dc=dc: E.dma_start(
                out=c.xmir[:, dc * TL:(dc + 1) * TL],
                in_=x_dram[dc * 128:(dc + 1) * 128, :]), ld[dc % 2], 16)
            if hooks and dc in hooks:
                hooks[dc]()
            P.wait("s", ld[dc % 2])
            sqb = c.sq
            if dc > 0:
                P.wait("s", pes, 2 * dc)
        else:
            P.wait("s", pre_sem, 2 * (dc + 1))
            sqb = c.sq if dc % 2 == 0 else c.mBb
            if dc >= 2:
                P.wait("s", pes, 2 * (dc - 1))
        P.opi("s", lambda E, dc=dc, sqb=sqb: E.activation(
            out=sqb[:], in_=c.xmir[:, dc * TL:(dc + 1) * TL], func=AF.Square), sqs, 1)
        if pre_sem is None:
            P.wait("t", ld[dc % 2])
        else:
            P.wait("t", pre_sem, 2 * (dc + 1))
        P.wait("t", sqs)
        for w in range(2):
            P.op("t", lambda E, dc=dc, w=w: E.matmul(
                s1b[w][0:1, :], c.ones_b[:, :],
                c.xmir[:, dc * TL + w * 512: dc * TL + (w + 1) * 512],
                start=(dc == 0), stop=(dc == NCH - 1)))
            P.opi("t", lambda E, dc=dc, w=w, sqb=sqb: E.matmul(
                s2b[w][0:1, :], c.ones_b[:, :],
                sqb[:, w * 512:(w + 1) * 512],
                start=(dc == 0), stop=(dc == NCH - 1)), pes, 1)
    P.wait("s", pes)
    for w in range(2):
        P.opi("s", lambda E, w=w: E.activation(
            out=c.m_row[0:1, w * 512:(w + 1) * 512], in_=s1b[w][0:1, :],
            func=AF.Identity, scale=c.consts[0:1, 3:4]), f"{tag}_r1", 1)
    P.wait("s", f"{tag}_r1")
    P.opi("s", lambda E: E.activation(out=c.t_row, in_=c.m_row,
                                      func=AF.Square), f"{tag}_r2", 1)
    P.wait("v", f"{tag}_r2")
    for w in range(2):
        P.opi("v", lambda E, w=w: E.scalar_tensor_tensor(
            out=c.r_row[0:1, w * 512:(w + 1) * 512], in0=s2b[w][0:1, :],
            scalar=c.consts[0:1, 3:4],
            in1=c.t_row[0:1, w * 512:(w + 1) * 512],
            op0=ALU.mult, op1=ALU.subtract), f"{tag}_r3", 1)
    P.wait("s", f"{tag}_r3")
    P.opi("s", lambda E: E.activation(out=c.t_row, in_=c.r_row,
                                      func=AF.Ln, bias=c.consts[0:1, 2:3]), f"{tag}_r4", 1)
    P.wait("s", f"{tag}_r4")
    P.opi("s", lambda E: E.activation(out=c.r_row, in_=c.t_row,
                                      func=AF.Exp, scale=c.consts[0:1, 4:5]), f"{tag}_r5", 1)
    P.wait("t", f"{tag}_r5")
    for w in range(2):
        P.opi("t", lambda E, w=w: E.matmul(
            s1b[w][:, :], c.ones_f[:, :],
            c.m_row[0:1, w * 512:(w + 1) * 512], start=True, stop=True), f"{tag}_bc", 1)
        P.opi("t", lambda E, w=w: E.matmul(
            s2b[w][:, :], c.ones_f[:, :],
            c.r_row[0:1, w * 512:(w + 1) * 512], start=True, stop=True), f"{tag}_bc", 1)
    P.wait("s", f"{tag}_bc")
    for w in range(2):
        P.opi("s", lambda E, w=w: E.copy(out=c.mBb[:, w * 512:(w + 1) * 512],
                                         in_=s1b[w][:, :]), f"{tag}_b2", 1)
        P.opi("s", lambda E, w=w: E.copy(out=c.rBf[:, w * 512:(w + 1) * 512],
                                         in_=s2b[w][:, :]), f"{tag}_b2", 1)
    lnd = f"{tag}_lnd"
    lns = f"{tag}_lns"
    P.wait("v", f"{tag}_b2")
    for dc in range(NCH):
        if dc > 0:
            P.wait("v", lnd, dc)
        P.opi("v", lambda E, dc=dc: E.tensor_sub(
            c.sq[:], c.xmir[:, dc * TL:(dc + 1) * TL], c.mBb[:]), lns, 1)
        P.wait("v", lns, dc + 1)
        P.opi("v", lambda E, dc=dc: E.tensor_mul(
            c.xmir[:, dc * TL:(dc + 1) * TL], c.sq[:], c.rBf[:]), lnd, 1)
    return lnd



def emit_proj_fm(c, tag, wT_dram, n_out_chunks, out_cb, gate_g=(),
                 in_sb=None, in_cols=TL, ln_sem=None, extra_chunk=None):
    P = c.P
    src = in_sb if in_sb is not None else c.xmir
    nw = in_cols // 512
    mm = f"{tag}_mm"
    gd = [f"{tag}_wl0", f"{tag}_wl1"]
    P.sem(f"{tag}_ev")
    rnd = 0
    for oc in range(n_out_chunks):
        buf = oc % 2
        if oc < 2:
            for sname, val in gate_g:
                P.wait("g", sname, val)
        if oc >= 2:
            P.wait("g", mm, (oc - 1) * nw)
        P.opi("g", lambda E, oc=oc, buf=buf: E.dma_start(
            out=c.wstrip[buf][:, 0:NCH * 128],
            in_=wT_dram[oc, :, :]), gd[buf], 16)
        P.wait("t", gd[buf])
        if ln_sem is not None:
            P.wait("t", ln_sem)
        for w in range(nw):
            ps = c.pp[(oc * nw + w) % 2]
            if rnd >= 2:
                P.wait("t", f"{tag}_ev", rnd - 1)
            rnd += 1
            for dc in range(NCH):
                last = (dc == NCH - 1) and (extra_chunk is None)
                fn = lambda E, w=w, dc=dc, buf=buf, ps=ps, last=last: E.matmul(
                    ps[:, :], c.wstrip[buf][:, dc * 128:(dc + 1) * 128],
                    src[:, dc * in_cols + w * 512: dc * in_cols + w * 512 + 512],
                    start=(dc == 0), stop=last)
                if last:
                    P.opi("t", fn, mm, 1)
                else:
                    P.op("t", fn)
            if extra_chunk is not None:
                P.opi("t", lambda E, oc=oc, w=w, ps=ps: extra_chunk(E, oc, w, ps), mm, 1)
            out_cb(oc, w, ps, P.cnt[mm])
    return mm


def make_evac_to_sbuf(c, tag, dst_sb, dst_col, bias_ap=None):
    P = c.P

    def cb(oc, w, ps, mmcnt):
        P.wait("s", f"{tag}_mm", mmcnt)
        col = dst_col(oc, w)
        if bias_ap is not None:
            P.opi("s", lambda E, ps=ps, col=col, oc=oc: E.activation(
                out=dst_sb[:, col:col + 512], in_=ps[:, :], func=AF.Identity,
                bias=bias_ap[:, oc:oc + 1]), f"{tag}_ev", 1)
        else:
            P.opi("s", lambda E, ps=ps, col=col: E.copy(
                out=dst_sb[:, col:col + 512], in_=ps[:, :]), f"{tag}_ev", 1)
    return cb


def make_evac_residual_store(c, tag, x_dram, xo_dram, bias_ap=None,
                             mir_sem=None, mir_gate=None):
    P = c.P
    state = {"r": 0}

    def cb(oc, w, ps, mmcnt):
        r = state["r"]
        state["r"] += 1
        par, ev = r % 2, r % 2
        xl = f"{tag}_xl{par}"
        st = f"{tag}_st{ev}"
        if r >= 2:
            P.wait("y", f"{tag}_res", r - 1)
        P.opi("y", lambda E, oc=oc, w=w, par=par: E.dma_start(
            out=c.xstg[par][:, :],
            in_=x_dram[oc * 128:(oc + 1) * 128, w * 512:(w + 1) * 512]), xl, 16)
        P.wait("s", f"{tag}_mm", mmcnt)
        if r >= 2:
            P.wait("s", f"{tag}_res", r - 1)
            P.wait("s", st, P.cnt.get(st, 0))
            if mir_sem is not None:
                P.wait("s", mir_sem, r - 1)
        if bias_ap is not None:
            P.opi("s", lambda E, ps=ps, oc=oc, ev=ev: E.activation(
                out=c.evac[ev][:, :], in_=ps[:, :], func=AF.Identity,
                bias=bias_ap[:, oc:oc + 1]), f"{tag}_ev", 1)
        else:
            P.opi("s", lambda E, ps=ps, ev=ev: E.copy(
                out=c.evac[ev][:, :], in_=ps[:, :]), f"{tag}_ev", 1)
        P.wait("v", f"{tag}_ev")
        P.wait("v", xl)
        P.opi("v", lambda E, ev=ev, par=par: E.tensor_add(
            c.evac[ev][:, :], c.evac[ev][:, :], c.xstg[par][:, :]), f"{tag}_res", 1)
        if mir_sem is not None:
            if r == 0 and mir_gate is not None:
                P.wait("v", *mir_gate)
            P.opi("v", lambda E, ev=ev, oc=oc, w=w: E.tensor_copy(
                c.xmir[:, oc * TL + w * 512: oc * TL + w * 512 + 512],
                c.evac[ev][:, :]), mir_sem, 1)
        P.wait("y", f"{tag}_res")
        P.opi("y", lambda E, oc=oc, w=w, ev=ev: E.dma_start(
            out=xo_dram[oc * 128:(oc + 1) * 128, w * 512:(w + 1) * 512],
            in_=c.evac[ev][:, :]), st, 16)
    return cb



def emit_l0a(c, io):
    nc, P = c.nc, c.P
    A = c.A
    kgb = A("l0_kgb", [128, 4096], BF16).ap()
    vgb = A("l0_vgb", [128, 32 * 128], BF16).ap()
    pb = A("l0_pb", [128, 32 * 256], BF16).ap()
    kcon = pb[:, 0:TL]
    vcon = pb[:, TL:TL + 8 * 128]
    mbuf = A("l0_mbuf", [128, N_MASKS_W * 256], BF16).ap()
    sel4 = A("l0_sel4", [128, 1], F32).ap()
    biasq = A("l0_biasq", [128, 32], F32).ap()
    biaso = A("l0_biaso", [128, 32], F32).ap()
    biask = A("l0_biask", [128, 1], F32).ap()
    kvrow = A("l0_kvrow", [1, 256], BF16).ap()

    P.opi("y", lambda E: E.dma_start(out=biasq[:], in_=io["wq_bT"][:, :]), "l0_sld", 16)
    P.opi("y", lambda E: E.dma_start(out=biaso[:], in_=io["wo_bT"][:, :]), "l0_sld", 16)
    P.opi("y", lambda E: E.dma_start(out=biask[:], in_=io["wk_bT"][:, :]), "l0_sld", 16)
    P.opi("y", lambda E: E.dma_start(out=kvrow[:], in_=io["kv_brow"][:, :]), "l0_sld", 16)
    P.opi("y", lambda E: E.dma_start(out=sel4[:, :], in_=io["sel4"][:, :]), "l0_sld", 16)
    P.wait("s", "l0_sld", 80)

    ln = emit_stats_ln(c, io["x"], "l0ln", hooks=getattr(c, "a1e_hooks", None))

    P.opi("g", lambda E: E.dma_start(
        out=c.wstrip[0][:, 0:NCH * 128], in_=io["wkT"][0, :, :]), "l0_gld", 16)
    P.opi("g", lambda E: E.dma_start(
        out=c.wstrip[1][:, 0:NCH * 128], in_=io["wvT"][0, :, :]), "l0_gld", 16)
    P.wait("t", "l0_gld", 32)
    P.wait("t", "l0_sld", 80)
    P.wait("t", ln)
    for w in range(2):
        for dc in range(NCH):
            fn = lambda E, w=w, dc=dc: E.matmul(
                c.ss[w][:, :], c.wstrip[0][:, dc * 128:(dc + 1) * 128],
                c.xmir[:, dc * TL + w * 512: dc * TL + w * 512 + 512],
                start=(dc == 0), stop=(dc == NCH - 1))
            if dc == NCH - 1:
                P.opi("t", fn, "l0_kmm", 1)
            else:
                P.op("t", fn)
    for w in range(2):
        P.wait("s", "l0_kmm", w + 1)
        P.opi("s", lambda E, w=w: E.activation(
            out=kcon[:, w * 512:(w + 1) * 512], in_=c.ss[w][:, :],
            func=AF.Identity, bias=biask[:, 0:1]), "l0_kev", 1)
    for tt in range(8):
        ps = c.pp[tt % 2]
        if tt >= 2:
            P.wait("t", "l0_vev", tt - 1)
        for dc in range(NCH):
            P.op("t", lambda E, tt=tt, dc=dc, ps=ps: E.matmul(
                ps[:, 0:128],
                c.xmir[:, dc * TL + tt * 128: dc * TL + tt * 128 + 128],
                c.wstrip[1][:, dc * 128:(dc + 1) * 128],
                start=(dc == 0), stop=False))
        P.opi("t", lambda E, ps=ps: E.matmul(
            ps[:, 0:128], c.ones_br[:, :], kvrow[0:1, 128:256],
            start=False, stop=True), "l0_vmm", 1)
        P.wait("s", "l0_vmm")
        P.opi("s", lambda E, tt=tt, ps=ps: E.copy(
            out=vcon[:, tt * 128:(tt + 1) * 128], in_=ps[:, 0:128]), "l0_vev", 1)

    P.wait("y", "l0_kev", 2)
    P.opi("y", lambda E: E.dma_start(out=io["kcon_d"][:, :], in_=kcon[:, :]), "l0_cst", 16)
    P.wait("y", "l0_vev", 8)
    P.opi("y", lambda E: E.dma_start(
        out=io["vcon_d"].rearrange("(t p) o -> p t o", p=128),
        in_=vcon.rearrange("p (t o) -> p t o", o=128)), "l0_cst", 16)
    P.wait("g", "l0_cst", 32)
    P.opi("g", lambda E: E.collective_compute(
        "AllGather", ALU.bypass, replica_groups=[[0, 1, 2, 3], [4, 5, 6, 7]],
        ins=[io["kcon_d"].opt()], outs=[io["kg_d"].opt()]), "l0_cc", 1)
    P.opi("g", lambda E: E.collective_compute(
        "AllGather", ALU.bypass, replica_groups=[[0, 1, 2, 3], [4, 5, 6, 7]],
        ins=[io["vcon_d"].opt()], outs=[io["vg_d"].opt()]), "l0_cc", 1)

    P.wait("t", "l0_vev", 8)
    if getattr(c, "has_a1e", False):
        P.wait("s", "g:a1vst", 16)
    qcb = make_evac_to_sbuf(c, "l0q", c.qb, lambda oc, w: oc * TL + w * 512,
                            bias_ap=biasq)
    emit_proj_fm(c, "l0q", io["wqT"], 32, qcb,
                 gate_g=(("l0_kmm", 2), ("l0_vmm", 8)), ln_sem=ln)

    P.wait("y", "l0_cc", 2)
    P.wait("y", "l0_cst", 32)
    for r in range(4):
        P.opi("y", lambda E, r=r: E.dma_start(
            out=kgb[:, r * TL:(r + 1) * TL],
            in_=io["kg_d"][r * 128:(r + 1) * 128, :]), "l0_kvg", 16)
        P.opi("y", lambda E, r=r: E.dma_start(
            out=vgb[:, r * 8 * 128:(r + 1) * 8 * 128].rearrange("p (t o) -> p t o", o=128),
            in_=io["vg_d"][r * TL:(r + 1) * TL, :].rearrange("(t p) o -> p t o", p=128)),
            "l0_kvg", 16)

    P.wait("t", "l0_kvg", 16 * 8)
    P.wait("t", "l0q_ev", 64)
    P.wait("t", "l0_kev", 2)
    P.wait("s", "l0_cst", 32)
    avs, exs, nms = "l0_av", "l0_ex", "l0_nm"
    for s in (avs, exs, nms):
        P.sem(s)
    ridx = 0
    pend = [None]
    prev_ngrp = [None]
    for w in range(4):
        chunks = S_W[w]
        nchk = len(chunks)
        P.wait("y", "l0_msk", P.cnt.get("l0_msk", 0))
        P.opi("y", lambda E, w=w: E.dma_start(
            out=mbuf.rearrange("p (m q) -> p m q", q=256),
            in_=io["masks0"][w * 8:(w + 1) * 8, :, :].rearrange("m p q -> p m q")),
            "l0_mld", 16)
        P.wait("v", "l0_mld")
        for h in range(32):
            qcol = h * TL + w * 256
            pav = c.pp[ridx % 2]
            pdn = c.pd[ridx % 2]
            P.wait("s", avs, P.cnt[avs])
            P.wait("t", nms, P.cnt[nms])

            def l0_av(i, j, last):
                if (w, j) in MASK_SLOT:
                    P.wait("t", "l0_msk", mskcnt[i])
                else:
                    P.wait("t", exs, excnt[i])
                fn = lambda E, i=i, j=j, pav=pav, last=last: E.matmul(
                    pav[:, 0:256], vgb[:, j * 128:(j + 1) * 128],
                    pb[:, i * 256:(i + 1) * 256], start=(i == 0), stop=last)
                if last:
                    P.opi("t", fn, avs, 1)
                else:
                    P.op("t", fn)

            nb = nchk // 4
            planned_dnb = set()

            def l0_dnb(j, pdn=pdn, nb=nb):
                planned_dnb.add(j)
                if j == 0:
                    P.wait("t", "l0_nbc", P.cnt.get("l0_nbc", 0))
                for u in range(4):
                    i = 4 * j + u
                    fn = lambda E, i=i, u=u, pdn=pdn, j=j, nb=nb: E.matmul(
                        pdn[32 * u:32 * u + 1, 0:256], c.ones_b[:, :],
                        pb[:, i * 256:(i + 1) * 256],
                        start=(j == 0), stop=(j == nb - 1),
                        tile_position=(0, 32 * u))
                    if j == nb - 1 and u == 3:
                        P.opi("t", fn, "l0_dnf", 1)
                    else:
                        P.op("t", fn)

            excnt = {}
            mskcnt = {}
            ngrp = nchk // 2
            for g in range(ngrp):
                ii = (2 * g, 2 * g + 1)
                bank = g % 4
                P.wait("t", exs, max(0, P.cnt[exs] - 3))
                for u, i in enumerate(ii):
                    j = chunks[i]
                    fn = lambda E, j=j, bank=bank, u=u, qcol=qcol: E.matmul(
                        c.ssb[:, bank * 512 + u * 256: bank * 512 + u * 256 + 256],
                        kgb[:, j * 128:(j + 1) * 128],
                        c.qb[:, qcol:qcol + 256], start=True, stop=True)
                    if u == 1:
                        P.opi("t", fn, "l0_smm", 1)
                    else:
                        P.op("t", fn)
                if prev_ngrp[0] is not None and g == max(0, prev_ngrp[0] - 2):
                    P.wait("s", "l0_dnf", P.cnt.get("l0_dnf", 0))
                P.wait("s", "l0_smm")
                ec = P.opi("s", lambda E, g=g, bank=bank: E.activation(
                    out=pb[:, 2 * g * 256: 2 * g * 256 + 512],
                    in_=c.ssb[:, bank * 512: bank * 512 + 512],
                    func=AF.Exp, bias=c.consts[:, 1:2], scale=c.consts[:, 0:1]), exs, 1)
                for i in ii:
                    excnt[i] = ec
                    j = chunks[i]
                    if (w, j) in MASK_SLOT:
                        ms = MASK_SLOT[(w, j)]
                        P.wait("v", exs)
                        mskcnt[i] = P.opi("v", lambda E, i=i, ms=ms: E.tensor_mul(
                            pb[:, i * 256:(i + 1) * 256], pb[:, i * 256:(i + 1) * 256],
                            mbuf[:, ms * 256:(ms + 1) * 256]), "l0_msk", 1)
                if g == 1 and pend[0] is not None:
                    pend[0]()
                    pend[0] = None
                if g >= 1:
                    l0_av(2 * (g - 1), chunks[2 * (g - 1)], False)
                    l0_av(2 * g - 1, chunks[2 * g - 1], False)
                    if g >= 2 and g % 2 == 0:
                        l0_dnb((g - 2) // 2)
            l0_av(nchk - 2, chunks[nchk - 2], False)
            l0_av(nchk - 1, chunks[nchk - 1], True)
            for j in range(nb):
                if j not in planned_dnb:
                    l0_dnb(j)
            avc = P.cnt[avs]
            dnfc = P.cnt["l0_dnf"]

            def mk_norm0(pav=pav, pdn=pdn, qcol=qcol, avc=avc, dnfc=dnfc):
                def plan():
                    P.wait("v", avs, avc)
                    P.wait("v", "l0_dnf", dnfc)
                    P.wait("v", nms, P.cnt[nms])
                    P.opi("v", lambda E: E.tensor_copy(
                        c.rBf[:, 0:256], pdn[:, 0:256]), "l0_dne", 1)
                    P.wait("t", "l0_dne")
                    P.opi("t", lambda E: E.matmul(
                        pdn[0:1, 0:256], sel4[:, :], c.rBf[:, 0:256],
                        start=True, stop=True), "l0_dsm", 1)
                    P.wait("v", "l0_dsm")
                    P.opi("v", lambda E: E.reciprocal(
                        out=c.rBf[0:1, 256:512], in_=pdn[0:1, 0:256]), "l0_rec", 1)
                    P.wait("t", "l0_rec")
                    P.opi("t", lambda E: E.matmul(
                        pdn[:, 256:512], c.ones_f[:, :], c.rBf[0:1, 256:512],
                        start=True, stop=True), "l0_bcm", 1)
                    P.wait("v", "l0_bcm")
                    P.opi("v", lambda E: E.tensor_copy(
                        c.rBf[:, 512:768], pdn[:, 256:512]), "l0_nbc", 1)
                    P.opi("v", lambda E: E.tensor_mul(
                        c.qb[:, qcol:qcol + 256], pav[:, 0:256],
                        c.rBf[:, 512:768]), nms, 1)
                return plan

            pend[0] = mk_norm0()
            prev_ngrp[0] = ngrp
            ridx += 1
    if pend[0] is not None:
        pend[0]()
        pend[0] = None

    P.wait("t", nms)
    P.wait("s", nms)
    fused = getattr(c, "has_a1e", False)
    ocb = make_evac_residual_store(c, "l0o", io["x"], io["xo"], bias_ap=biaso,
                                   mir_sem="g:mir0" if fused else None,
                                   mir_gate=("l0q_mm", 64))
    emit_proj_fm(c, "l0o", io["woT"], 32, ocb, gate_g=(("l0q_mm", 64),),
                 in_sb=c.qb)
    P.wait("y", "l0o_st0", P.cnt["l0o_st0"])
    P.wait("y", "l0o_st1", P.cnt["l0o_st1"])


def build_l0a():
    nc = bass.Bass("TRN2", target_bir_lowering=False)
    io = {}
    def din(name, shape, dt=F32):
        io[name] = nc.dram_tensor(name, shape, dt, kind="ExternalInput").ap()
    din("x", [D, TL])
    din("wqT", [32, 128, D]); din("wkT", [1, 128, D]); din("wvT", [1, 128, D]); din("woT", [32, 128, D])
    din("wq_bT", [128, 32]); din("wo_bT", [128, 32]); din("wk_bT", [128, 1])
    din("kv_brow", [1, 256], BF16)
    din("masks0", [32, 128, 256], BF16)
    din("sel4", [128, 1])
    io["xo"] = nc.dram_tensor("xo", [D, TL], F32, kind="ExternalOutput").ap()
    io["kcon_d"] = nc.dram_tensor("kcon_d", [128, TL], BF16).ap()
    io["vcon_d"] = nc.dram_tensor("vcon_d", [TL, 128], BF16).ap()
    io["kg_d"] = nc.dram_tensor("kg_d", [512, TL], BF16).ap()
    io["vg_d"] = nc.dram_tensor("vg_d", [4096, 128], BF16).ap()
    P = Prog(nc)
    c = alloc_common(nc, P)
    emit_l0a(c, io)
    P.finalize()
    return nc



def emit_l0f(c, io):
    nc, P = c.nc, c.P
    A = c.A
    wrt = A("f0_wrt", [128, NCH * 8], BF16).ap()
    bstrip = A("f0_bstrip", [128, NCH * 128], BF16).ap()
    hb = A("f0_hb", [128, TL], BF16).ap()
    cB = A("f0_cB", [128, TL], BF16).ap()
    ctm = A("f0_ctm", [128, 8], BF16).ap()
    ctT = A("f0_ctT", [8, TL], BF16).ap()
    Esb = A("f0_E", [8, 128], BF16).ap()
    idn = A("f0_idn", [128, 128], BF16).ap()
    rsc = A("f0_rsc", [128, 32], F32).ap()
    zt = A("f0_zt", [128, 8], F32).ap()
    wzt = A("f0_wzt", [128, 8], F32).ap()
    biasb = A("f0_biasb", [128, 32], F32).ap()
    pdb = c.pd[0].bitcast(BF16)

    P.opi("y", lambda E: E.dma_start(out=biasb[:], in_=io["b_baseT"][:, :]), "f0_sld", 16)
    P.opi("y", lambda E: E.dma_start(out=Esb[:], in_=io["Econst"][:, :]), "f0_sld", 16)
    P.opi("y", lambda E: E.dma_start(out=idn[:], in_=io["ident"][:, :]), "f0_sld", 16)
    P.opi("g", lambda E: E.dma_start(
        out=wrt.rearrange("p (k e) -> p k e", e=8),
        in_=io["wrT"].rearrange("(k p) e -> p k e", p=128)), "f0_gld", 16)
    P.opi("g", lambda E: E.dma_start(out=bstrip[:, :], in_=io["BfT"][:, :]), "f0_gld", 16)
    P.opi("g", lambda E: E.dma_start(
        out=c.wstrip[0][:, 0:NCH * 128], in_=io["AT"][0, :, :]), "f0_gld", 16)

    ln = emit_stats_ln(c, io["x"], "f0ln",
                       pre_sem="g:mir0" if getattr(c, "has_a1e", False) else None)

    P.wait("t", ln)
    P.wait("t", "f0_sld", 48)
    P.wait("t", "f0_gld", 48)
    for qt in range(8):
        ps = c.ss[qt % 2]
        if qt >= 2:
            P.wait("t", "f0_ct", qt - 1)
        for dc in range(NCH):
            fn = lambda E, qt=qt, dc=dc, ps=ps: E.matmul(
                ps[:, 0:8], c.xmir[:, dc * TL + qt * 128: dc * TL + qt * 128 + 128],
                wrt[:, dc * 8:(dc + 1) * 8],
                start=(dc == 0), stop=(dc == NCH - 1))
            if dc == NCH - 1:
                P.opi("t", fn, "f0_rmm", 1)
            else:
                P.op("t", fn)
        P.wait("v", "f0_rmm")
        if qt > 0:
            P.wait("v", "f0_ctv", qt)
        P.opi("v", lambda E, ps=ps: E.reduce_max(
            rsc[:, 0:1], ps[:, 0:8], axis=mybir.AxisListType.X), "f0_t1", 1)
        P.wait("v", "f0_t1")
        P.opi("v", lambda E: E.tensor_scalar(
            rsc[:, 1:2], rsc[:, 0:1], -1.0, None, ALU.mult), "f0_t2", 1)
        P.wait("s", "f0_t2")
        P.opi("s", lambda E, ps=ps: E.activation(
            out=zt[:], in_=ps[:, 0:8], func=AF.Exp,
            bias=rsc[:, 1:2]), "f0_t3", 1)
        P.wait("v", "f0_t2")
        P.opi("v", lambda E, ps=ps: E.tensor_scalar(
            rsc[:, 8:16], ps[:, 0:8], rsc[:, 0:1], None, ALU.is_ge), "f0_t4", 1)
        P.wait("v", "f0_t4")
        P.opi("v", lambda E, ps=ps: E.scalar_tensor_tensor(
            out=rsc[:, 16:24], in0=rsc[:, 8:16], scalar=-1e30, in1=ps[:, 0:8],
            op0=ALU.mult, op1=ALU.add), "f0_t5", 1)
        P.wait("v", "f0_t5")
        P.opi("v", lambda E: E.reduce_max(
            rsc[:, 4:5], rsc[:, 16:24], axis=mybir.AxisListType.X), "f0_t6", 1)
        P.wait("v", "f0_t6")
        P.opi("v", lambda E, ps=ps: E.tensor_scalar(
            rsc[:, 8:16], ps[:, 0:8], rsc[:, 4:5], None, ALU.is_ge), "f0_t7", 1)
        P.wait("v", "f0_t7")
        P.wait("v", "f0_t3")
        P.opi("v", lambda E: E.tensor_mul(wzt[:], zt[:], rsc[:, 8:16]), "f0_t8", 1)
        P.wait("v", "f0_t8")
        P.opi("v", lambda E: E.reduce_sum(
            rsc[:, 2:3], wzt[:], axis=mybir.AxisListType.X), "f0_t9", 1)
        P.wait("v", "f0_t9")
        P.opi("v", lambda E: E.reciprocal(rsc[:, 3:4], rsc[:, 2:3]), "f0_t10", 1)
        P.wait("v", "f0_t10")
        P.opi("v", lambda E: E.tensor_scalar(
            ctm[:], wzt[:], rsc[:, 3:4], 2.0, ALU.mult, ALU.mult), "f0_ctv", 1)
        P.wait("t", "f0_ctv")
        if qt > 0:
            P.wait("t", "f0_cte", qt)
        P.opi("t", lambda E, qt=qt: E.transpose(
            pdb[0:8, qt * 128:(qt + 1) * 128], ctm[:], idn[:]), "f0_ct", 1)
        P.wait("s", "f0_ct")
        P.opi("s", lambda E, qt=qt: E.copy(
            out=ctT[:, qt * 128:(qt + 1) * 128],
            in_=pdb[0:8, qt * 128:(qt + 1) * 128]), "f0_cte", 1)
    P.wait("t", "f0_cte", 8)
    for w in range(2):
        P.opi("t", lambda E, w=w: E.matmul(
            c.pt[w][:, :], Esb[:, :], ctT[:, w * 512:(w + 1) * 512],
            start=True, stop=True), "f0_ex", 1)
        P.wait("s", "f0_ex")
        P.opi("s", lambda E, w=w: E.copy(
            out=cB[:, w * 512:(w + 1) * 512], in_=c.pt[w][:, :]), "f0_exe", 1)

    for w in range(2):
        ps = c.pp[w]
        for dc in range(NCH):
            fn = lambda E, w=w, dc=dc, ps=ps: E.matmul(
                ps[:, :], c.wstrip[0][:, dc * 128:(dc + 1) * 128],
                c.xmir[:, dc * TL + w * 512: dc * TL + w * 512 + 512],
                start=(dc == 0), stop=(dc == NCH - 1))
            if dc == NCH - 1:
                P.opi("t", fn, "f0_hmm", 1)
            else:
                P.op("t", fn)
        P.wait("s", "f0_hmm")
        P.opi("s", lambda E, w=w, ps=ps: E.copy(
            out=hb[:, w * 512:(w + 1) * 512], in_=ps[:, :]), "f0_hev", 1)
    P.wait("v", "f0_hev", 2)
    P.wait("v", "f0_exe", 2)
    for w in range(2):
        P.opi("v", lambda E, w=w: E.tensor_mul(
            hb[:, w * 512:(w + 1) * 512], hb[:, w * 512:(w + 1) * 512],
            cB[:, w * 512:(w + 1) * 512]), "f0_hw", 1)

    P.wait("t", "f0_hw", 2)
    P.wait("t", "f0_hev", 2)

    def extra(E, oc, w, ps):
        return E.matmul(ps[:, :], bstrip[:, oc * 128:(oc + 1) * 128],
                        hb[:, w * 512:(w + 1) * 512], start=False, stop=True)

    P.wait("s", "f0_sld", 48)
    ocb = make_evac_residual_store(c, "f0o", io["x"], io["xo"], bias_ap=biasb)
    emit_proj_fm(c, "f0o", io["wbT"], 32, ocb, gate_g=(("f0_hmm", 2),),
                 extra_chunk=extra)
    P.wait("y", "f0o_st0", P.cnt["f0o_st0"])
    P.wait("y", "f0o_st1", P.cnt["f0o_st1"])


def build_l0f():
    nc = bass.Bass("TRN2", target_bir_lowering=False)
    io = {}
    def din(name, shape, dt=F32):
        io[name] = nc.dram_tensor(name, shape, dt, kind="ExternalInput").ap()
    din("x", [D, TL])
    din("wbT", [32, 128, D]); din("wrT", [D, 8]); din("AT", [1, 128, D]); din("BfT", [128, D])
    din("b_baseT", [128, 32])
    din("Econst", [8, 128], BF16)
    din("ident", [128, 128], BF16)
    io["xo"] = nc.dram_tensor("xo", [D, TL], F32, kind="ExternalOutput").ap()
    P = Prog(nc)
    c = alloc_common(nc, P)
    emit_l0f(c, io)
    P.finalize()
    return nc



def l1_k_loads(kvh):
    out = []
    for r in range(8):
        out.append(("kgc_d", 512 * r, r * 1024 + kvh * 128, 0, 512))
    for r in range(4):
        out.append(("kgn_d", 4096 + 1024 * r, r * 1024 + kvh * 128, 0, 1024))
    return out


def l1_v_loads(kvh):
    out = []
    for r in range(8):
        out.append(("vgc_d", 512 * r, r * 512, 512))
    for r in range(4):
        out.append(("vgn_d", 4096 + 1024 * r, r * 1024, 1024))
    return out



def emit_a1_head(c, io):
    nc, P = c.nc, c.P
    qb_addr = nc.lookup_mloc("qb").addr
    gK = nc.alloc_sbuf_tensor_at("a1e_gK", [128, 4096], F32, offset=qb_addr + 16384).ap()
    ab = nc.lookup_mloc("arena").addr
    c.a1e_vstg = nc.alloc_sbuf_tensor_at("a1e_vstg", [128, TL], BF16, offset=ab + 40000).ap()
    c.a1e_ktr = nc.alloc_sbuf_tensor_at("a1e_ktr", [128, TL], BF16, offset=ab + 42048).ap()
    c.a1e_idn = nc.alloc_sbuf_tensor_at("a1e_idn", [128, 128], BF16, offset=ab + 44096).ap()
    eidx = nc.alloc_sbuf_tensor_at("a1e_eidx", [128, 2], I32, offset=ab + 44352).ap()
    selc = nc.alloc_sbuf_tensor_at("a1e_selc", [128, 2], I32, offset=ab + 44384).ap()

    P.opi("y", lambda E: E.dma_start(out=selc[:, :], in_=io["selconst"][:, :]), "g:a1sld", 16)
    P.opi("y", lambda E: E.dma_start(out=c.a1e_idn[:], in_=io["ident"][:, :]), "g:a1sld", 16)
    P.wait("g", "g:a1sld", 16)
    P.opi("g", lambda E: E.indirect_dma_start(
        out=eidx[:, 0:1], out_offset=None,
        in_=io["blockidx"][0:2, :],
        in_offset=bass.IndirectOffsetOnAxis(selc[:, 0:1], 0)), "g:a1gi", 16)
    P.wait("v", "g:a1gi", 16)
    for dbl in range(6):
        src_col, dst_col = (0, 1) if dbl % 2 == 0 else (1, 0)
        P.opi("v", lambda E, s=src_col, d=dst_col: E.tensor_add(
            eidx[:, d:d + 1], eidx[:, s:s + 1], eidx[:, s:s + 1]), "g:a1ei", 1)
        P.wait("v", "g:a1ei", dbl + 1)
    P.opi("v", lambda E: E.tensor_add(eidx[:, 1:2], eidx[:, 0:1], selc[:, 1:2]), "g:a1ei", 1)
    P.wait("v", "g:a1ei", 7)
    P.opi("v", lambda E: E.tensor_copy(eidx[:, 0:1], eidx[:, 1:2]), "g:a1ei", 1)
    P.wait("g", "g:a1ei", 8)
    P.opi("g", lambda E: E.indirect_dma_start(
        out=gK[:, :], out_offset=None,
        in_=io["kcache"][:, :],
        in_offset=bass.IndirectOffsetOnAxis(eidx[:, 0:1], 0)), "g:a1gk", 16)

    def h_restage():
        P.wait("g", "g:a1gk", 16)
        P.opi("g", lambda E: E.dma_start(
            out=io["kstage"].rearrange("(p j) c -> p j c", j=4),
            in_=gK.rearrange("p (j c) -> p j c", c=1024)), "g:a1gk", 16)

    def h_gatherv():
        P.wait("g", "g:a1gk", 32)
        P.opi("g", lambda E: E.indirect_dma_start(
            out=gK[:, :], out_offset=None,
            in_=io["vcache"][:, :],
            in_offset=bass.IndirectOffsetOnAxis(eidx[:, 0:1], 0)), "g:a1gv", 16)

    def h_storev():
        P.wait("g", "g:a1gv", 16)
        P.opi("g", lambda E: E.dma_start(
            out=io["vcach_d"].rearrange("(p j) c -> p j c", j=4),
            in_=gK.rearrange("p (j c) -> p j c", c=1024)), "g:a1vst", 16)

    c.a1e_hooks = {8: h_restage, 16: h_gatherv, 24: h_storev}


def emit_a1_tail(c, io):
    nc, P = c.nc, c.P
    vstg, ktr, idn = c.a1e_vstg, c.a1e_ktr, c.a1e_idn
    for tt in range(4):
        P.wait("y", "g:a1gk", 32)
        if tt > 0:
            P.wait("y", "g:a1kt", 8 * tt)
        P.opi("y", lambda E, tt=tt: E.dma_start(
            out=vstg.rearrange("p (kv h) -> p kv h", h=128),
            in_=io["kstage"][tt * 128:(tt + 1) * 128, :].rearrange(
                "p (kv h) -> p kv h", h=128)), "g:a1ks", 16)
        P.wait("t", "g:a1ks")
        if tt > 0:
            P.wait("t", "g:a1kte", 8 * tt)
            P.wait("s", "g:a1kst", 16 * tt)
        for kv in range(8):
            if tt * 8 + kv >= 2:
                P.wait("t", "g:a1kte", tt * 8 + kv - 1)
            P.opi("t", lambda E, kv=kv: E.transpose(
                c.pd[kv % 2].bitcast(BF16)[:, (kv // 2) * 128:(kv // 2) * 128 + 128],
                vstg[:, kv * 128:(kv + 1) * 128], idn[:]), "g:a1kt", 1)
            P.wait("s", "g:a1kt")
            P.opi("s", lambda E, kv=kv: E.copy(
                out=ktr[:, kv * 128:(kv + 1) * 128],
                in_=c.pd[kv % 2].bitcast(BF16)[:, (kv // 2) * 128:(kv // 2) * 128 + 128]),
                "g:a1kte", 1)
        P.wait("y", "g:a1kte")
        P.opi("y", lambda E, tt=tt: E.dma_start(
            out=io["kcach_d"][:, tt * 128:(tt + 1) * 128].rearrange(
                "(kv h) t -> h kv t", h=128),
            in_=ktr.rearrange("h (kv t) -> h kv t", t=128)), "g:a1kst", 16)
    P.wait("g", "g:a1kst", 64)
    P.wait("g", "g:a1vst", 16)
    P.opi("g", lambda E: E.collective_compute(
        "AllGather", ALU.bypass, replica_groups=[[0, 1, 2, 3, 4, 5, 6, 7]],
        ins=[io["kcach_d"].opt()], outs=[io["kgc_d"].opt()]), "g:a1ecc", 1)
    P.opi("g", lambda E: E.collective_compute(
        "AllGather", ALU.bypass, replica_groups=[[0, 1, 2, 3, 4, 5, 6, 7]],
        ins=[io["vcach_d"].opt()], outs=[io["vgc_d"].opt()]), "g:a1ecc", 1)


def emit_l1a(c, io):
    import os
    CUT = int(os.environ.get("L1A_CUT", "99"))
    nc, P = c.nc, c.P
    A = c.A
    kkvh = A("a1_kkv", [128, 8192], BF16)
    vkvh = A("a1_vkv", [128, 8192], BF16)
    kkv = kkvh.ap()
    vkv = vkvh.ap()
    pbh = A("a1_pb", [128, 8 * 512], BF16)
    pb = pbh.ap()
    rtb = A("a1_rt", [128, 128], BF16).ap()
    idn = A("a1_idn", [128, 128], BF16).ap()
    pb_base = nc.lookup_mloc(pbh).addr
    vstg = nc.alloc_sbuf_tensor_at("a1_vstg", [128, TL], BF16, offset=pb_base).ap()
    ktr = nc.alloc_sbuf_tensor_at("a1_ktr", [128, TL], BF16, offset=pb_base + 2048).ap()
    qtmp = [nc.alloc_sbuf_tensor_at(f"a1_qt{i}", [128, 512], BF16,
                                    offset=pb_base + 4096 + i * 1024).ap()
            for i in range(2)]
    sel4 = A("a1_sel4", [128, 1], F32).ap()
    rsc = A("a1_rsc", [128, 256], F32).ap()
    ropecs = A("a1_ropecs", [128, 2], F32).ap()

    vfm = nc.alloc_sbuf_tensor_at("a1_vfm", [128, 8192], BF16,
                                  offset=nc.lookup_mloc(kkvh).addr).ap()
    kfm = nc.alloc_sbuf_tensor_at("a1_kfm", [128, 8192], BF16,
                                  offset=nc.lookup_mloc(vkvh).addr).ap()

    P.opi("y", lambda E: E.dma_start(out=idn[:], in_=io["ident"][:, :]), "a1_sld", 16)
    P.opi("y", lambda E: E.dma_start(out=ropecs[:, :], in_=io["ropecs"][:, :]), "a1_sld", 16)
    P.opi("y", lambda E: E.dma_start(out=sel4[:, :], in_=io["sel4"][:, :]), "a1_sld", 16)

    if CUT <= 1:
        return
    ln = emit_stats_ln(c, io["x"], "a1ln")

    P.opi("y", lambda E: E.dma_start(out=rtb[:], in_=io["m1c"][:, :]), "a1_rld", 16)
    P.opi("y", lambda E: E.dma_start(out=ktr[:, 0:128], in_=io["m2c"][:, :]), "a1_rld", 16)
    P.wait("v", "a1_rld", 32)
    P.wait("v", "a1_sld", 48)
    P.opi("v", lambda E: E.tensor_scalar(
        rsc[:, 0:128], rtb[:, :], ropecs[:, 0:1], None, ALU.mult), "a1_r1", 1)
    P.wait("v", "a1_r1")
    P.opi("v", lambda E: E.scalar_tensor_tensor(
        out=rsc[:, 128:256], in0=ktr[:, 0:128], scalar=ropecs[:, 1:2],
        in1=rsc[:, 0:128], op0=ALU.mult, op1=ALU.add), "a1_r2", 1)
    P.wait("v", "a1_r2")
    P.opi("v", lambda E: E.tensor_copy(rtb[:, :], rsc[:, 128:256]), "a1_rt", 1)

    if CUT <= 2:
        return
    vcb = make_evac_to_sbuf(c, "a1v", vfm, lambda oc, w: oc * TL + w * 512)
    emit_proj_fm(c, "a1v", io["wvT"], 8, vcb, ln_sem=ln)
    P.wait("t", "a1v_ev", 16)
    P.wait("t", "a1_sld", 48)
    for tt in range(8):
        if tt > 0:
            P.wait("t", "a1_vte", 8 * tt)
            P.wait("s", "a1_vst", 16 * tt)
        for oc in range(8):
            if tt * 8 + oc >= 2:
                P.wait("t", "a1_vte", tt * 8 + oc - 1)
            P.opi("t", lambda E, tt=tt, oc=oc: E.transpose(
                c.pd[oc % 2].bitcast(BF16)[:, (oc // 2) * 128:(oc // 2) * 128 + 128],
                vfm[:, oc * TL + tt * 128: oc * TL + tt * 128 + 128], idn[:]), "a1_vt", 1)
            P.wait("s", "a1_vt")
            P.opi("s", lambda E, oc=oc: E.copy(
                out=vstg[:, oc * 128:(oc + 1) * 128],
                in_=c.pd[oc % 2].bitcast(BF16)[:, (oc // 2) * 128:(oc // 2) * 128 + 128]),
                "a1_vte", 1)
        P.wait("y", "a1_vte")
        P.opi("y", lambda E, tt=tt: E.dma_start(
            out=io["vnew_d"][tt * 128:(tt + 1) * 128, :],
            in_=vstg[:, :]), "a1_vst", 16)
    P.wait("t", "a1_rt")

    def kcb(oc, w, ps, mmcnt):
        P.wait("s", "a1k_mm", mmcnt)
        qi = (oc * 2 + w) % 2
        if oc * 2 + w >= 2:
            P.wait("s", "a1_krm", oc * 2 + w - 1)
        P.opi("s", lambda E, ps=ps, qi=qi: E.copy(
            out=qtmp[qi][:, :], in_=ps[:, :]), "a1k_qc", 1)
        P.wait("t", "a1k_qc")
        P.opi("t", lambda E, ps=ps, qi=qi: E.matmul(
            ps[:, :], rtb[:, :], qtmp[qi][:, :], start=True, stop=True), "a1_krm", 1)
        P.wait("s", "a1_krm")
        P.opi("s", lambda E, ps=ps, oc=oc, w=w: E.copy(
            out=kfm[:, oc * TL + w * 512: oc * TL + w * 512 + 512],
            in_=ps[:, :]), "a1k_ev", 1)

    emit_proj_fm(c, "a1k", io["wkT"], 8, kcb, gate_g=(("a1v_mm", 16),), ln_sem=ln)
    P.wait("y", "a1k_ev", 16)
    for kv in range(8):
        P.opi("y", lambda E, kv=kv: E.dma_start(
            out=io["knew_d"][kv * 128:(kv + 1) * 128, :],
            in_=kfm[:, kv * TL:(kv + 1) * TL]), "a1_knst", 16)

    if CUT <= 3:
        return
    P.wait("g", "a1_vst", 16 * 8)
    P.wait("g", "a1_knst", 16 * 8)
    P.opi("g", lambda E: E.collective_compute(
        "AllGather", ALU.bypass, replica_groups=[[0, 1, 2, 3, 4, 5, 6, 7]],
        ins=[io["knew_d"].opt()], outs=[io["kgn_d"].opt()]), "a1_cc", 1)
    P.opi("g", lambda E: E.collective_compute(
        "AllGather", ALU.bypass, replica_groups=[[0, 1, 2, 3, 4, 5, 6, 7]],
        ins=[io["vnew_d"].opt()], outs=[io["vgn_d"].opt()]), "a1_cc", 1)

    if CUT <= 4:
        return
    def qcb(oc, w, ps, mmcnt):
        P.wait("s", "a1q_mm", mmcnt)
        r = oc * 2 + w
        qi = r % 2
        if r >= 2:
            P.wait("s", "a1_qrm", r - 1)
        else:
            P.wait("s", "a1_krm", 16)
        P.opi("s", lambda E, ps=ps, qi=qi: E.copy(
            out=qtmp[qi][:, :], in_=ps[:, :]), "a1q_qc", 1)
        P.wait("t", "a1q_qc")
        P.opi("t", lambda E, ps=ps, qi=qi: E.matmul(
            ps[:, :], rtb[:, :], qtmp[qi][:, :], start=True, stop=True), "a1_qrm", 1)
        P.wait("s", "a1_qrm")
        P.opi("s", lambda E, ps=ps, oc=oc, w=w: E.copy(
            out=c.qb[:, oc * TL + w * 512: oc * TL + w * 512 + 512],
            in_=ps[:, :]), "a1q_ev", 1)

    emit_proj_fm(c, "a1q", io["wqT"], 32, qcb, gate_g=(("a1k_mm", 16),), ln_sem=ln)

    if CUT <= 5:
        return
    P.wait("y", "a1_cc", 2)
    P.wait("y", "g:a1ecc", 2)
    avs, exs, nms = "a1_av", "a1_ex", "a1_nm"
    for s in (avs, exs, nms):
        P.sem(s)
    P.sem("a1_kvl")
    ridx = 0
    pend = [None]
    for half in range(2):
        for kvh in range(8):
            if ridx > 0:
                P.wait("y", avs, P.cnt[avs])
                P.wait("y", "a1_smm", P.cnt["a1_smm"])
            for (tn, kc, row, col, n) in l1_k_loads(kvh):
                P.opi("y", lambda E, tn=tn, kc=kc, row=row, col=col, n=n: E.dma_start(
                    out=kkv[:, kc:kc + n],
                    in_=io[tn][row:row + 128, col:col + n]), "a1_kvl", 16)
            for (tn, vc, row, n) in l1_v_loads(kvh):
                P.opi("y", lambda E, tn=tn, vc=vc, row=row, n=n, kvh=kvh: E.dma_start(
                    out=vkv[:, vc:vc + n].rearrange("p (t h) -> p t h", h=128),
                    in_=io[tn][row:row + n, kvh * 128:(kvh + 1) * 128].rearrange(
                        "(t p) h -> p t h", p=128)), "a1_kvl", 16)
            P.wait("t", "a1_kvl")
            if ridx == 0:
                P.wait("t", "a1q_ev", 64)
            for qh4 in range(4):
                h = kvh * 4 + qh4
                qcol = h * TL + half * 512
                pav = c.pp[ridx % 2]
                pdn = c.pd[ridx % 2]
                P.wait("s", avs, P.cnt[avs])
                P.wait("t", nms, P.cnt[nms])

                def a1_av(i, last):
                    P.wait("t", exs, excnt[i])
                    fn = lambda E, i=i, pav=pav, last=last: E.matmul(
                        pav[:, :], vkv[:, i * 128:(i + 1) * 128],
                        pb[:, (i % 8) * 512:(i % 8) * 512 + 512],
                        start=(i == 0), stop=last)
                    if last:
                        P.opi("t", fn, avs, 1)
                    else:
                        P.op("t", fn)

                def a1_dnb(j, pdn=pdn):
                    if j == 0:
                        P.wait("t", "a1_nbc", P.cnt.get("a1_nbc", 0))
                    for u in range(4):
                        i = 4 * j + u
                        fn = lambda E, i=i, u=u, pdn=pdn, j=j: E.matmul(
                            pdn[32 * u:32 * u + 1, :], c.ones_b[:, :],
                            pb[:, (i % 8) * 512:(i % 8) * 512 + 512],
                            start=(j == 0), stop=(j == 15),
                            tile_position=(0, 32 * u))
                        if j == 15 and u == 3:
                            P.opi("t", fn, "a1_dnf", 1)
                        else:
                            P.op("t", fn)

                excnt = {}
                for g in range(32):
                    sshalf = (g % 2) * 1024
                    P.wait("t", exs, max(0, P.cnt[exs] - 1))
                    for u in range(2):
                        i = 2 * g + u
                        fn = lambda E, i=i, sshalf=sshalf, u=u, qcol=qcol: E.matmul(
                            c.ssb[:, sshalf + u * 512: sshalf + u * 512 + 512],
                            kkv[:, i * 128:(i + 1) * 128],
                            c.qb[:, qcol:qcol + 512], start=True, stop=True)
                        if u == 1:
                            P.opi("t", fn, "a1_smm", 1)
                        else:
                            P.op("t", fn)
                    if g == 2:
                        P.wait("s", "a1_dnf", P.cnt.get("a1_dnf", 0))
                    P.wait("s", "a1_smm")
                    ec = P.opi("s", lambda E, g=g, sshalf=sshalf: E.activation(
                        out=pb[:, (g % 4) * 1024:(g % 4) * 1024 + 1024],
                        in_=c.ssb[:, sshalf:sshalf + 1024],
                        func=AF.Exp, bias=c.consts[:, 1:2], scale=c.consts[:, 0:1]), exs, 1)
                    excnt[2 * g] = ec
                    excnt[2 * g + 1] = ec
                    if g == 1 and pend[0] is not None:
                        pend[0]()
                        pend[0] = None
                    if g >= 1:
                        a1_av(2 * (g - 1), False)
                        a1_av(2 * g - 1, False)
                        if g >= 2 and g % 2 == 0:
                            a1_dnb((g - 2) // 2)
                a1_av(62, False)
                a1_av(63, True)
                a1_dnb(15)
                avc = P.cnt[avs]
                dnfc = P.cnt["a1_dnf"]

                def mk_norm(pav=pav, pdn=pdn, qcol=qcol, avc=avc, dnfc=dnfc):
                    def plan():
                        P.wait("v", avs, avc)
                        P.wait("v", "a1_dnf", dnfc)
                        P.wait("v", nms, P.cnt[nms])
                        P.opi("v", lambda E: E.tensor_copy(
                            c.rBf[:, 0:512], pdn[:, 0:512]), "a1_dne", 1)
                        P.wait("t", "a1_dne")
                        P.opi("t", lambda E: E.matmul(
                            pdn[0:1, 0:512], sel4[:, :], c.rBf[:, 0:512],
                            start=True, stop=True), "a1_dsm", 1)
                        P.wait("v", "a1_dsm")
                        P.opi("v", lambda E: E.reciprocal(
                            out=c.rBf[0:1, 512:1024], in_=pdn[0:1, 0:512]), "a1_rec", 1)
                        P.wait("t", "a1_rec")
                        P.opi("t", lambda E: E.matmul(
                            pdn[:, :], c.ones_f[:, :], c.rBf[0:1, 512:1024],
                            start=True, stop=True), "a1_bcm", 1)
                        P.wait("v", "a1_bcm")
                        P.opi("v", lambda E: E.tensor_copy(
                            c.rBf[:, 0:512], pdn[:, :]), "a1_nbc", 1)
                        P.opi("v", lambda E: E.tensor_mul(
                            c.qb[:, qcol:qcol + 512], pav[:, :], c.rBf[:, 0:512]), nms, 1)
                    return plan

                pend[0] = mk_norm()
                ridx += 1
    if pend[0] is not None:
        pend[0]()
        pend[0] = None

    if CUT <= 6:
        return
    P.wait("t", nms)
    fused = getattr(c, "has_a1e", False)
    ocb = make_evac_residual_store(c, "a1o", io["x"], io["xo"],
                                   mir_sem="g:mir2" if fused else None,
                                   mir_gate=("a1q_mm", 64))
    emit_proj_fm(c, "a1o", io["woT"], 32, ocb, gate_g=(("a1q_mm", 64),),
                 in_sb=c.qb)
    P.wait("y", "a1o_st0", P.cnt["a1o_st0"])
    P.wait("y", "a1o_st1", P.cnt["a1o_st1"])


def build_l1a():
    raise NotImplementedError("standalone l1a removed; cache staging is fused "
                              "into the l0a phase (emit_a1_head/emit_a1_tail)")



def emit_l1f(c, io):
    nc, P = c.nc, c.P
    A = c.A
    c0 = A("f1_c0", [128, NCH * 96], F32).ap()
    m0 = A("f1_m0", [128, NCH * 96], mybir.dt.uint8).ap()
    c1 = A("f1_c1", [128, NCH * 24], F32).ap()
    m1 = A("f1_m1", [128, NCH * 24], mybir.dt.uint8).ap()
    wsum = A("f1_wsum", [128, NCH * 40], BF16).ap()
    wqt = A("f1_wqt", [40, D], BF16).ap()
    encb = A("f1_encb", [128, NCH * 40], BF16).ap()
    eb = A("f1_eb", [40, TL], BF16).ap()
    msc = A("f1_msc", [128, 96], F32).ap()
    biasb = A("f1_biasb", [128, 32], F32).ap()

    P.opi("y", lambda E: E.dma_start(out=biasb[:], in_=io["b_baseT"][:, :]), "f1_sld", 16)
    P.opi("y", lambda E: E.dma_start(
        out=c0.rearrange("p (k r) -> p k r", r=96),
        in_=io["c0r"].rearrange("(k p) r -> p k r", p=128)), "f1_sld", 16)
    P.opi("y", lambda E: E.dma_start(
        out=m0.rearrange("p (k r) -> p k r", r=96),
        in_=io["m0r"].rearrange("(k p) r -> p k r", p=128)), "f1_sld", 16)
    P.opi("y", lambda E: E.dma_start(
        out=c1.rearrange("p (k r) -> p k r", r=24),
        in_=io["c1r"].rearrange("(k p) r -> p k r", p=128)), "f1_sld", 16)
    P.opi("y", lambda E: E.dma_start(
        out=m1.rearrange("p (k r) -> p k r", r=24),
        in_=io["m1r"].rearrange("(k p) r -> p k r", p=128)), "f1_sld", 16)
    P.opi("g", lambda E: E.dma_start(
        out=encb.rearrange("p (k r) -> p k r", r=40),
        in_=io["encT"].rearrange("(k p) r -> p k r", p=128)), "f1_gld", 16)

    ln = emit_stats_ln(c, io["x"], "f1ln",
                       pre_sem="g:mir2" if getattr(c, "has_a1e", False) else None)

    P.wait("v", "f1_sld", 80)
    for oc in range(NCH):
        if oc > 0:
            P.wait("v", "f1_ws", 2 * oc)
        P.opi("v", lambda E, oc=oc: E.tensor_copy(
            msc[:, 0:96], m0[:, oc * 96:(oc + 1) * 96]), "f1_wa", 1)
        P.wait("v", "f1_wa")
        P.opi("v", lambda E, oc=oc: E.tensor_mul(
            msc[:, 0:96], msc[:, 0:96], c0[:, oc * 96:(oc + 1) * 96]), "f1_wb", 1)
        P.wait("v", "f1_wb")
        P.opi("v", lambda E: E.tensor_add(
            msc[:, 0:32], msc[:, 0:32], msc[:, 32:64]), "f1_wc", 1)
        P.wait("v", "f1_wc")
        P.opi("v", lambda E: E.tensor_add(
            msc[:, 0:32], msc[:, 0:32], msc[:, 64:96]), "f1_wd", 1)
        P.wait("v", "f1_wd")
        P.opi("v", lambda E, oc=oc: E.tensor_scalar(
            wsum[:, oc * 40:oc * 40 + 32], msc[:, 0:32], 0.5, None, ALU.mult),
            "f1_ws", 1)
        P.wait("v", "f1_ws", 2 * oc + 1)
        P.opi("v", lambda E, oc=oc: E.tensor_copy(
            msc[:, 0:24], m1[:, oc * 24:(oc + 1) * 24]), "f1_we", 1)
        P.wait("v", "f1_we")
        P.opi("v", lambda E, oc=oc: E.tensor_mul(
            msc[:, 0:24], msc[:, 0:24], c1[:, oc * 24:(oc + 1) * 24]), "f1_wf", 1)
        P.wait("v", "f1_wf")
        P.opi("v", lambda E: E.tensor_add(
            msc[:, 0:8], msc[:, 0:8], msc[:, 8:16]), "f1_wg", 1)
        P.wait("v", "f1_wg")
        P.opi("v", lambda E: E.tensor_add(
            msc[:, 0:8], msc[:, 0:8], msc[:, 16:24]), "f1_wh", 1)
        P.wait("v", "f1_wh")
        P.opi("v", lambda E, oc=oc: E.tensor_scalar(
            wsum[:, oc * 40 + 32:oc * 40 + 40], msc[:, 0:8], 0.5, None, ALU.mult),
            "f1_ws", 1)

    banks = [c.pp[0], c.pp[1], c.ss[0], c.ss[1], c.pd[0], c.pd[1], c.pt[0], c.pt[1]]
    P.wait("t", "f1ln_r3", 2)
    P.wait("t", "f1ln_b2", 4)
    gd = ["f1_qw0", "f1_qw1"]
    for oc in range(NCH):
        buf = oc % 2
        if oc >= 2:
            P.wait("y", "f1_qmm", (oc - 1) * 8)
        P.opi("y", lambda E, oc=oc, buf=buf: E.dma_start(
            out=c.wstrip[buf][:, 0:4096],
            in_=io["QT"][oc * 128:(oc + 1) * 128, :]), gd[buf], 16)
        P.wait("t", gd[buf])
        P.wait("t", "f1_ws", 2 * (oc + 1))
        for tw in range(8):
            fn = lambda E, oc=oc, buf=buf, tw=tw: E.matmul(
                banks[tw][0:40, :], wsum[:, oc * 40:(oc + 1) * 40],
                c.wstrip[buf][:, tw * 512:(tw + 1) * 512],
                start=(oc == 0), stop=(oc == NCH - 1))
            P.opi("t", fn, "f1_qmm", 1)
    P.wait("s", "f1_qmm", NCH * 8)
    for tw in range(8):
        P.opi("s", lambda E, tw=tw: E.copy(
            out=wqt[:, tw * 512:(tw + 1) * 512], in_=banks[tw][0:40, :]), "f1_qev", 1)

    P.wait("t", "f1_qev", 8)
    P.wait("t", ln)
    P.wait("t", "f1_gld", 16)
    for w in range(2):
        ps = c.pp[w]
        for dc in range(NCH):
            fn = lambda E, w=w, dc=dc, ps=ps: E.matmul(
                ps[0:40, :], encb[:, dc * 40:(dc + 1) * 40],
                c.xmir[:, dc * TL + w * 512: dc * TL + w * 512 + 512],
                start=(dc == 0), stop=(dc == NCH - 1))
            if dc == NCH - 1:
                P.opi("t", fn, "f1_emm", 1)
            else:
                P.op("t", fn)
        P.wait("s", "f1_emm")
        P.opi("s", lambda E, w=w, ps=ps: E.copy(
            out=eb[:, w * 512:(w + 1) * 512], in_=ps[0:40, :]), "f1_eev", 1)

    P.wait("t", "f1_eev", 2)

    def extra(E, oc, w, ps):
        return E.matmul(ps[:, :], wqt[:, oc * 128:(oc + 1) * 128],
                        eb[:, w * 512:(w + 1) * 512], start=False, stop=True)

    P.wait("s", "f1_sld", 80)
    ocb = make_evac_residual_store(c, "f1o", io["x"], io["xo"], bias_ap=biasb)
    emit_proj_fm(c, "f1o", io["wbT"], 32, ocb,
                 gate_g=(("f1_emm", 2), ("f1_qmm", NCH * 8)),
                 extra_chunk=extra)
    P.wait("y", "f1o_st0", P.cnt["f1o_st0"])
    P.wait("y", "f1o_st1", P.cnt["f1o_st1"])


def build_l1f():
    nc = bass.Bass("TRN2", target_bir_lowering=False)
    io = {}
    def din(name, shape, dt=F32):
        io[name] = nc.dram_tensor(name, shape, dt, kind="ExternalInput").ap()
    din("x", [D, TL])
    din("wbT", [32, 128, D]); din("b_baseT", [128, 32])
    din("c0r", [D, 96]); din("c1r", [D, 24])
    din("m0r", [D, 96], mybir.dt.uint8); din("m1r", [D, 24], mybir.dt.uint8)
    din("encT", [D, 40])
    din("QT", [D, D], BF16)
    io["xo"] = nc.dram_tensor("xo", [D, TL], F32, kind="ExternalOutput").ap()
    P = Prog(nc)
    c = alloc_common(nc, P)
    emit_l1f(c, io)
    P.finalize()
    return nc



ARENA_BYTES = 45056

PHASE_IN_SPECS = {
    "l0a": [("wqT", [32, 128, D], F32), ("wkT", [1, 128, D], F32), ("wvT", [1, 128, D], F32),
            ("woT", [32, 128, D], F32), ("wq_bT", [128, 32], F32), ("wo_bT", [128, 32], F32),
            ("wk_bT", [128, 1], F32), ("kv_brow", [1, 256], BF16),
            ("masks0", [32, 128, 256], BF16), ("sel4", [128, 1], F32)],
    "l0f": [("wbT", [32, 128, D], F32), ("wrT", [D, 8], F32), ("AT", [1, 128, D], F32),
            ("BfT", [128, D], F32), ("b_baseT", [128, 32], F32),
            ("Econst", [8, 128], BF16), ("ident", [128, 128], BF16)],
    "l1a": [("wqT", [32, 128, D], F32), ("wkT", [8, 128, D], F32), ("wvT", [8, 128, D], F32),
            ("woT", [32, 128, D], F32), ("kcache", [8192, 4096], F32),
            ("vcache", [8192, 4096], F32), ("blockidx", [2, 1], I32),
            ("selconst", [128, 2], I32), ("ropecs", [128, 2], F32),
            ("m1c", [128, 128], BF16), ("m2c", [128, 128], BF16),
            ("ident", [128, 128], BF16), ("sel4", [128, 1], F32)],
    "l1f": [("wbT", [32, 128, D], F32), ("b_baseT", [128, 32], F32), ("c0r", [D, 96], F32),
            ("c1r", [D, 24], F32), ("m0r", [D, 96], mybir.dt.uint8),
            ("m1r", [D, 24], mybir.dt.uint8), ("encT", [D, 40], F32),
            ("QT", [D, D], BF16)],
}

PHASE_INTERNALS = {
    "l0a": [("kcon_d", [128, TL], BF16), ("vcon_d", [TL, 128], BF16),
            ("kg_d", [512, TL], BF16), ("vg_d", [4096, 128], BF16)],
    "l0f": [],
    "l1a": [("kstage", [512, 1024], BF16), ("kcach_d", [1024, 512], BF16),
            ("vcach_d", [512, 1024], BF16), ("kgc_d", [8192, 512], BF16),
            ("vgc_d", [4096, 1024], BF16), ("knew_d", [1024, 1024], BF16),
            ("vnew_d", [1024, 1024], BF16), ("kgn_d", [8192, 1024], BF16),
            ("vgn_d", [8192, 1024], BF16)],
    "l1f": [],
}

PHASE_EMIT = {"l0a": emit_l0a, "l0f": emit_l0f, "l1a": emit_l1a, "l1f": emit_l1f}
PHASE_ST = {"l0a": "l0o", "l0f": "f0o", "l1a": "a1o", "l1f": "f1o"}


def build_fused():
    nc = bass.Bass("TRN2", target_bir_lowering=False)
    P = Prog(nc)
    c = alloc_common(nc, P)
    arena = nc.alloc_sbuf_tensor("arena", [128, ARENA_BYTES], mybir.dt.uint8)
    arena_base = nc.lookup_mloc("arena").addr

    xin = nc.dram_tensor("xin", [D, TL], F32, kind="ExternalInput").ap()
    xout = nc.dram_tensor("xout", [D, TL], F32, kind="ExternalOutput").ap()
    xi = [nc.dram_tensor(f"xi{i}", [D, TL], F32).ap() for i in range(3)]
    xs_chain = [xin, xi[0], xi[1], xi[2], xout]

    io_l1a = {}
    for tname, shape, dt in PHASE_IN_SPECS["l1a"]:
        io_l1a[tname] = nc.dram_tensor(f"l1a_{tname}", shape, dt,
                                       kind="ExternalInput").ap()
    for tname, shape, dt in PHASE_INTERNALS["l1a"]:
        io_l1a[tname] = nc.dram_tensor(f"l1a_{tname}", shape, dt).ap()
    c.has_a1e = True
    emit_a1_head(c, io_l1a)

    for pi, name in enumerate(("l0a", "l0f", "l1a", "l1f")):
        state = {"off": arena_base}

        def A(tname, shape, dtype, state=state, name=name):
            import functools
            import operator
            nbytes = functools.reduce(operator.mul, shape[1:], 1) * mybir.dt.size(dtype)
            nbytes = (nbytes + 31) & ~31
            h = nc.alloc_sbuf_tensor_at(f"{name}_{tname}", shape, dtype,
                                        offset=state["off"])
            state["off"] += nbytes
            assert state["off"] <= arena_base + ARENA_BYTES, \
                f"arena overflow in {name}: {state['off'] - arena_base}"
            return h
        c.A = A
        if name == "l1a":
            io = io_l1a
        else:
            io = {}
            for tname, shape, dt in PHASE_IN_SPECS[name]:
                io[tname] = nc.dram_tensor(f"{name}_{tname}", shape, dt,
                                           kind="ExternalInput").ap()
            for tname, shape, dt in PHASE_INTERNALS[name]:
                io[tname] = nc.dram_tensor(f"{name}_{tname}", shape, dt).ap()
        io["x"] = xs_chain[pi]
        io["xo"] = xs_chain[pi + 1]
        if pi > 0:
            prev = PHASE_ST[("l0a", "l0f", "l1a", "l1f")[pi - 1]]
            engs = ("g", "y") if name in ("l0f", "l1f") else ("t", "v", "s", "g", "y")
            for e in engs:
                P.wait(e, f"{prev}_st0", P.cnt[f"{prev}_st0"])
                P.wait(e, f"{prev}_st1", P.cnt[f"{prev}_st1"])
        if name == "l0f":
            P.wait("t", "g:a1kte", 32)
        P.new_scope()
        PHASE_EMIT[name](c, io)
        if name == "l0a":
            emit_a1_tail(c, io_l1a)
    P.finalize()
    return nc



def tile_strips(wT):
    n_oc = wT.shape[1] // 128
    a = wT.reshape(32, 128, n_oc, 128)
    return np.ascontiguousarray(a.transpose(2, 1, 0, 3).reshape(n_oc, 128, 4096))


def shard_x(x):
    out = []
    for core in range(8):
        b, cc = divmod(core, 4)
        rows = np.concatenate([x[b, t * 128:(t + 1) * 128, :] for t in OWNED[cc]], axis=0)
        out.append(np.ascontiguousarray(rows.T))
    return out


def unshard_x(shards):
    x = np.empty((2, 4096, 4096), dtype=np.float32)
    for core in range(8):
        b, cc = divmod(core, 4)
        rows = shards[core].T
        for li, t in enumerate(OWNED[cc]):
            x[b, t * 128:(t + 1) * 128, :] = rows[li * 128:(li + 1) * 128, :]
    return x


def build_l0_masks():
    per_cc = []
    for cc in range(4):
        m = np.zeros((32, 128, 256), dtype=np.float32)
        for w in range(4):
            g0, g1 = OWNED[cc][2 * w], OWNED[cc][2 * w + 1]
            for j in MASKED_W[w]:
                slot = w * 8 + MASK_SLOT[(w, j)]
                gk = GK[j]
                kidx = gk * 128 + np.arange(128)[:, None]
                m[slot, :, 0:128] = (kidx <= g0 * 128 + np.arange(128)[None, :])
                m[slot, :, 128:256] = (kidx <= g1 * 128 + np.arange(128)[None, :])
        per_cc.append(m.astype(ml_dtypes.bfloat16))
    return per_cc


def l0a_in_maps(inputs, xs):
    masks = build_l0_masks()
    f = np.float32
    wqT = tile_strips(np.asarray(inputs["L0_attn_wq_w"], f).T)
    wkT = tile_strips(np.asarray(inputs["L0_attn_wk_w"], f).T)
    wvT = tile_strips(np.asarray(inputs["L0_attn_wv_w"], f).T)
    woT = tile_strips(np.asarray(inputs["L0_attn_wo_w"], f).T)
    wq_bT = np.ascontiguousarray(np.asarray(inputs["L0_attn_wq_b"], f).reshape(32, 128).T)
    wo_bT = np.ascontiguousarray(np.asarray(inputs["L0_attn_wo_b"], f).reshape(32, 128).T)
    wk_bT = np.ascontiguousarray(np.asarray(inputs["L0_attn_wk_b"], f).reshape(128, 1))
    kv_brow = np.concatenate([
        np.asarray(inputs["L0_attn_wk_b"], f), np.asarray(inputs["L0_attn_wv_b"], f)
    ]).astype(ml_dtypes.bfloat16).reshape(1, 256)
    sel4 = np.zeros((128, 1), np.float32)
    sel4[[0, 32, 64, 96], 0] = 1.0
    maps = []
    for core in range(8):
        cc = core % 4
        maps.append({
            "x": xs[core], "wqT": wqT, "wkT": wkT, "wvT": wvT, "woT": woT,
            "wq_bT": wq_bT, "wo_bT": wo_bT, "wk_bT": wk_bT, "kv_brow": kv_brow,
            "masks0": masks[cc], "sel4": sel4,
        })
    return maps


def l0f_in_maps(inputs, xs):
    f = np.float32
    wbT = tile_strips(np.asarray(inputs["L0_ffn_W_base"], f).T)
    wrT = np.ascontiguousarray(np.asarray(inputs["L0_ffn_W_router"], f).T)
    A = np.asarray(inputs["L0_ffn_A"], f)
    B = np.asarray(inputs["L0_ffn_B"], f)
    AT = tile_strips(np.ascontiguousarray(A.reshape(128, 4096).T))
    BfT = np.ascontiguousarray(B.transpose(0, 2, 1).reshape(128, 4096))
    b_baseT = np.ascontiguousarray(np.asarray(inputs["L0_ffn_b_base"], f).reshape(32, 128).T)
    E = np.zeros((8, 128), np.float32)
    for e in range(8):
        E[e, e * 16:(e + 1) * 16] = 1.0
    Econst = E.astype(ml_dtypes.bfloat16)
    ident = np.eye(128, dtype=ml_dtypes.bfloat16)
    maps = []
    for core in range(8):
        maps.append({"x": xs[core], "wbT": wbT, "wrT": wrT, "AT": AT, "BfT": BfT,
                     "b_baseT": b_baseT, "Econst": Econst, "ident": ident})
    return maps


def l1a_in_maps(inputs, xs):
    f = np.float32
    wqT = tile_strips(np.asarray(inputs["L1_attn_wq"], f).T)
    wkT = tile_strips(np.asarray(inputs["L1_attn_wk"], f).T)
    wvT = tile_strips(np.asarray(inputs["L1_attn_wv"], f).T)
    woT = tile_strips(np.asarray(inputs["L1_attn_wo"], f).T)
    kcache = np.ascontiguousarray(np.asarray(inputs["L1_attn_k_cache"], f).reshape(8192, 4096))
    vcache = np.ascontiguousarray(np.asarray(inputs["L1_attn_v_cache"], f).reshape(8192, 4096))
    bi = np.sort(np.asarray(inputs["L1_attn_block_idx"]).astype(np.int32))
    cos = np.asarray(inputs["L1_attn_freqs_cos"], f)[0]
    sin = np.asarray(inputs["L1_attn_freqs_sin"], f)[0]
    ropecs = np.stack([np.repeat(cos, 2), np.repeat(sin, 2)], axis=1).astype(f)
    selconst = np.stack([np.arange(128) // 64, np.arange(128) % 64], axis=1).astype(np.int32)
    m1c = np.eye(128, dtype=ml_dtypes.bfloat16)
    m2 = np.zeros((128, 128), np.float32)
    for i in range(64):
        m2[2 * i, 2 * i + 1] = 1.0
        m2[2 * i + 1, 2 * i] = -1.0
    m2c = m2.astype(ml_dtypes.bfloat16)
    ident = np.eye(128, dtype=ml_dtypes.bfloat16)
    sel4 = np.zeros((128, 1), np.float32)
    sel4[[0, 32, 64, 96], 0] = 1.0
    maps = []
    for core in range(8):
        myblk = np.ascontiguousarray(bi[2 * core:2 * core + 2].reshape(2, 1))
        maps.append({"x": xs[core], "wqT": wqT, "wkT": wkT, "wvT": wvT, "woT": woT,
                     "kcache": kcache, "vcache": vcache, "blockidx": myblk,
                     "selconst": selconst, "ropecs": ropecs, "m1c": m1c,
                     "m2c": m2c, "ident": ident, "sel4": sel4})
    return maps


_QT_CACHE = {}


def fourier_qt():
    if "qt" not in _QT_CACHE:
        import math
        n = 4096
        t = np.arange(n)
        Q = np.zeros((n, n), np.float64)
        Q[:, 0] = 1.0 / math.sqrt(n)
        f_ = np.arange(1, n // 2)
        ang = 2.0 * np.pi * np.outer(t, f_) / n
        Q[:, 1:n - 1:2] = np.cos(ang) * math.sqrt(2.0 / n)
        Q[:, 2:n - 1:2] = np.sin(ang) * math.sqrt(2.0 / n)
        Q[:, n - 1] = np.where(t % 2 == 0, 1.0, -1.0) / math.sqrt(n)
        _QT_CACHE["qt"] = np.ascontiguousarray(Q.T).astype(ml_dtypes.bfloat16)
    return _QT_CACHE["qt"]


def l1f_in_maps(inputs, xs):
    f = np.float32
    wbT = tile_strips(np.asarray(inputs["L1_ffn_W_base"], f).T)
    b_baseT = np.ascontiguousarray(np.asarray(inputs["L1_ffn_b_base"], f).reshape(32, 128).T)
    c0r = np.ascontiguousarray(np.asarray(inputs["L1_ffn_coefs0"], f).transpose(1, 0, 2).reshape(4096, 96))
    c1r = np.ascontiguousarray(np.asarray(inputs["L1_ffn_coefs1"], f).transpose(1, 0, 2).reshape(4096, 24))
    m0r = np.ascontiguousarray(np.asarray(inputs["L1_ffn_mask0"]).transpose(1, 0, 2).reshape(4096, 96).astype(np.uint8))
    m1r = np.ascontiguousarray(np.asarray(inputs["L1_ffn_mask1"]).transpose(1, 0, 2).reshape(4096, 24).astype(np.uint8))
    encT = np.ascontiguousarray(np.concatenate([
        np.asarray(inputs["L1_ffn_enc0"], f),
        np.asarray(inputs["L1_ffn_enc1"], f)], axis=0).T)
    QT = fourier_qt()
    maps = []
    for core in range(8):
        maps.append({"x": xs[core], "wbT": wbT, "b_baseT": b_baseT,
                     "c0r": c0r, "c1r": c1r, "m0r": m0r, "m1r": m1r,
                     "encT": encT, "QT": QT})
    return maps


_NC_CACHE = {}


def _get_nc(name, builder):
    if name not in _NC_CACHE:
        _NC_CACHE[name] = builder()
    return _NC_CACHE[name]


PHASE_MAPFNS = {"l0a": None, "l0f": None, "l1a": None, "l1f": None}


def fused_in_maps(inputs, xs):
    per_phase = {n: PHASE_MAPFNS[n](inputs, xs) for n in PHASE_MAPFNS}
    merged = []
    for core in range(8):
        m = {"xin": xs[core]}
        for pname, pmaps in per_phase.items():
            for k, v in pmaps[core].items():
                if k == "x":
                    continue
                m[f"{pname}_{k}"] = v
        merged.append(m)
    return merged


PHASES = [
    ("l0a", build_l0a, l0a_in_maps),
    ("l0f", build_l0f, l0f_in_maps),
    ("l1a", build_l1a, l1a_in_maps),
    ("l1f", build_l1f, l1f_in_maps),
]

import os as _os
RUN_PHASES = _os.environ.get("KPHASES", "l0a,l0f,l1a,l1f").split(",")
KMODE = _os.environ.get("KMODE", "fused")

PHASE_MAPFNS.update({"l0a": l0a_in_maps, "l0f": l0f_in_maps,
                     "l1a": l1a_in_maps, "l1f": l1f_in_maps})


def kernel(**inputs):
    x = np.asarray(inputs["L0_attn_x"], dtype=np.float32)
    xs = shard_x(x)
    if KMODE == "fused":
        nc = _get_nc("fused", build_fused)
        res = run_bass_kernel_spmd(nc, fused_in_maps(inputs, xs),
                                   core_ids=list(range(8)))
        return unshard_x([np.ascontiguousarray(np.asarray(res.results[i]["xout"]))
                          for i in range(8)])
    for name, builder, mapfn in PHASES:
        if name not in RUN_PHASES:
            continue
        nc = _get_nc(name, builder)
        res = run_bass_kernel_spmd(nc, mapfn(inputs, xs), core_ids=list(range(8)))
        xs = [np.ascontiguousarray(np.asarray(res.results[i]["xo"])) for i in range(8)]
    return unshard_x(xs)


# version 18 (final)
# speedup vs baseline: 1.0312x; 1.0312x over previous
import numpy as np
import ml_dtypes

import concourse.bass as bass
import concourse.mybir as mybir
from concourse.bass_utils import run_bass_kernel_spmd

F32 = mybir.dt.float32
BF16 = mybir.dt.bfloat16
I32 = mybir.dt.int32
AF = mybir.ActivationFunctionType
ALU = mybir.AluOpType

D = 4096
NCH = 32
TL = 1024
EXP_SHIFT = 10.0
EPS = 1e-5
INV_SQRT_HD = float(1.0 / np.sqrt(128.0))


def owned_tiles(cc):
    return sorted([2 * cc, 2 * cc + 1, 14 - 2 * cc, 15 - 2 * cc,
                   16 + 2 * cc, 17 + 2 * cc, 30 - 2 * cc, 31 - 2 * cc])

OWNED = {cc: owned_tiles(cc) for cc in range(4)}
GK = [OWNED[r][jj] for r in range(4) for jj in range(8)]
G1MAX = [7, 15, 23, 31]
G0MIN = [0, 8, 16, 24]
S_W = [[j for j in range(32) if GK[j] <= G1MAX[w]] for w in range(4)]
MASKED_W = [[j for j in S_W[w] if GK[j] >= G0MIN[w]] for w in range(4)]
MASK_SLOT = {}
for _w in range(4):
    for _k, _j in enumerate(MASKED_W[_w]):
        MASK_SLOT[(_w, _j)] = _k
N_MASKS_W = 8



class Prog:

    def __init__(self, nc):
        self.nc = nc
        self.ops = {e: [] for e in ("t", "v", "s", "g", "y")}
        self.cnt = {}
        self.phys = []
        self.pcnt = []
        self.map = {}
        self.pool_ptr = 0
        self.last_wait = {}
        self.gmap = {}
        self.gcnt = {}

    def new_scope(self):
        self.map = {}
        self.cnt = {}
        self.pool_ptr = 0

    def _entry(self, name):
        if name not in self.map:
            i = self.pool_ptr
            self.pool_ptr += 1
            if i >= len(self.phys):
                self.phys.append(self.nc.alloc_semaphore(f"S{i}"))
                self.pcnt.append(0)
            self.map[name] = (i, self.pcnt[i])
            self.cnt[name] = 0
        return self.map[name]

    def _gentry(self, name):
        if name not in self.gmap:
            self.gmap[name] = self.nc.alloc_semaphore(f"SG{len(self.gmap)}")
            self.gcnt[name] = 0
        return self.gmap[name]

    def sem(self, name):
        if name.startswith("g:"):
            return self._gentry(name)
        i, _ = self._entry(name)
        return self.phys[i]

    def op(self, eng, fn):
        self.ops[eng].append(fn)

    def opi(self, eng, fn, sname, by=1):
        if sname.startswith("g:"):
            s = self._gentry(sname)
            self.gcnt[sname] += by
            self.ops[eng].append(lambda E, fn=fn, s=s, by=by: fn(E).then_inc(s, by))
            return self.gcnt[sname]
        i, _ = self._entry(sname)
        self.cnt[sname] += by
        self.pcnt[i] += by
        s = self.phys[i]
        self.ops[eng].append(lambda E, fn=fn, s=s, by=by: fn(E).then_inc(s, by))
        return self.cnt[sname]

    def wait(self, eng, sname, val=None):
        if sname.startswith("g:"):
            s = self._gentry(sname)
            vv = self.gcnt[sname] if val is None else val
            if vv <= 0:
                return
            key = (eng, sname)
        else:
            i, carry = self._entry(sname)
            v = self.cnt[sname] if val is None else val
            if v <= 0:
                return
            s = self.phys[i]
            vv = carry + v
            key = (eng, i)
        if self.last_wait.get(key, -1) >= vv:
            return
        self.last_wait[key] = vv
        self.ops[eng].append(lambda E, s=s, v=vv: E.wait_ge(s, v))

    def finalize(self):
        with self.nc.Block() as block:
            def mk(ops):
                def run(E):
                    for fn in ops:
                        fn(E)
                return run
            block.tensor(mk(self.ops["t"]))
            block.vector(mk(self.ops["v"]))
            block.scalar(mk(self.ops["s"]))
            block.gpsimd(mk(self.ops["g"]))
            block.sync(mk(self.ops["y"]))


class Ctx:
    pass


def alloc_common(nc, P):
    c = Ctx()
    c.nc = nc
    c.P = P
    A = nc.alloc_sbuf_tensor
    c.xmir = A("xmir", [128, NCH * TL], BF16).ap()
    c.qb = A("qb", [128, 32 * TL], BF16).ap()
    c.wstrip = [A(f"wstrip{i}", [128, NCH * 128], BF16).ap() for i in range(2)]
    c.xstg = [A(f"xstg{i}", [128, 512], F32).ap() for i in range(2)]
    c.evac = [A(f"evac{i}", [128, 512], F32).ap() for i in range(2)]
    c.sq = A("sq", [128, TL], BF16).ap()
    c.rows = nc.alloc_sbuf_tensor_at("rows", [1, 2048], F32,
                                     offset=nc.lookup_mloc("qb").addr).ap()
    c.m_row = c.rows[0:1, 0:1024]
    c.r_row = c.rows[0:1, 1024:2048]
    c.mBb = A("mBb", [128, TL], BF16).ap()
    c.rBf = A("rBf", [128, TL], F32).ap()
    c.t_row = c.rBf[0:1, 0:1024]
    c.dn_row = c.rBf[0:1, 0:512]
    c.dn2_row = c.rBf[0:1, 512:1024]
    c.nrm = [A(f"nrm{i}", [128, 256], F32).ap() for i in range(2)]
    c.consts = A("consts", [128, 8], F32).ap()
    c.C_SCALE, c.C_SHIFT, c.C_EPS, c.C_INVD, c.C_NEGH = 0, 1, 2, 3, 4
    c.ones_b = A("ones_b", [128, 1], BF16).ap()
    c.ones_f = A("ones_f", [1, 128], F32).ap()
    c.ones_fc = A("ones_fc", [128, 1], F32).ap()
    c.ones_br = A("ones_br", [1, 128], BF16).ap()

    PA = nc.alloc_psum_tensor
    c.pp = [PA(f"pp{i}", [128, 512], F32).ap() for i in range(2)]
    c.pd = [PA(f"pd{i}", [128, 512], F32).ap() for i in range(2)]
    c.ssb = PA("ssb", [128, 2048], F32).ap()
    c.ss = [c.ssb[:, 0:512], c.ssb[:, 512:1024]]
    c.pt = [c.ssb[:, 1024:1536], c.ssb[:, 1536:2048]]

    c.nrm2 = A("nrm2", [128, 256], F32).ap()
    c.A = nc.alloc_sbuf_tensor

    def init(v):
        v.memset(c.consts[:, 0:1], INV_SQRT_HD)
        v.memset(c.consts[:, 1:2], -EXP_SHIFT)
        v.memset(c.consts[:, 2:3], EPS)
        v.memset(c.consts[:, 3:4], 1.0 / 4096.0)
        v.memset(c.consts[:, 4:5], -0.5)
        v.memset(c.ones_b[:], 1.0)
        v.memset(c.ones_f[:], 1.0)
        v.memset(c.ones_fc[:], 1.0)
        v.memset(c.pd[0][:, :], 0.0)
        v.memset(c.pd[1][:, :], 0.0)
        return v.memset(c.ones_br[:], 1.0)
    P.opi("v", init, "init", 1)
    for e in ("t", "s", "g", "y"):
        P.wait(e, "init", 1)
    return c



def emit_stats_ln(c, x_dram, tag, hooks=None, pre_sem=None):
    P = c.P
    sqs, pes = f"{tag}_sq", f"{tag}_pe"
    ld = [f"{tag}_ld0", f"{tag}_ld1"]
    s1b = [c.pt[0], c.ss[0]]
    s2b = [c.pt[1], c.ss[1]]
    for dc in range(NCH):
        if pre_sem is None:
            if dc >= 2:
                P.wait("g", sqs, dc - 1)
            P.opi("g", lambda E, dc=dc: E.dma_start(
                out=c.xmir[:, dc * TL:(dc + 1) * TL],
                in_=x_dram[dc * 128:(dc + 1) * 128, :]), ld[dc % 2], 16)
            if hooks and dc in hooks:
                hooks[dc]()
            P.wait("s", ld[dc % 2])
            sqb = c.sq
            if dc > 0:
                P.wait("s", pes, 2 * dc)
        else:
            P.wait("s", pre_sem, 2 * (dc + 1))
            sqb = c.sq if dc % 2 == 0 else c.mBb
            if dc >= 2:
                P.wait("s", pes, 2 * (dc - 1))
        P.opi("s", lambda E, dc=dc, sqb=sqb: E.activation(
            out=sqb[:], in_=c.xmir[:, dc * TL:(dc + 1) * TL], func=AF.Square), sqs, 1)
        if pre_sem is None:
            P.wait("t", ld[dc % 2])
        else:
            P.wait("t", pre_sem, 2 * (dc + 1))
        P.wait("t", sqs)
        for w in range(2):
            P.op("t", lambda E, dc=dc, w=w: E.matmul(
                s1b[w][0:1, :], c.ones_b[:, :],
                c.xmir[:, dc * TL + w * 512: dc * TL + (w + 1) * 512],
                start=(dc == 0), stop=(dc == NCH - 1)))
            P.opi("t", lambda E, dc=dc, w=w, sqb=sqb: E.matmul(
                s2b[w][0:1, :], c.ones_b[:, :],
                sqb[:, w * 512:(w + 1) * 512],
                start=(dc == 0), stop=(dc == NCH - 1)), pes, 1)
    P.wait("s", pes)
    for w in range(2):
        P.opi("s", lambda E, w=w: E.activation(
            out=c.m_row[0:1, w * 512:(w + 1) * 512], in_=s1b[w][0:1, :],
            func=AF.Identity, scale=c.consts[0:1, 3:4]), f"{tag}_r1", 1)
    P.wait("s", f"{tag}_r1")
    P.opi("s", lambda E: E.activation(out=c.t_row, in_=c.m_row,
                                      func=AF.Square), f"{tag}_r2", 1)
    P.wait("v", f"{tag}_r2")
    for w in range(2):
        P.opi("v", lambda E, w=w: E.scalar_tensor_tensor(
            out=c.r_row[0:1, w * 512:(w + 1) * 512], in0=s2b[w][0:1, :],
            scalar=c.consts[0:1, 3:4],
            in1=c.t_row[0:1, w * 512:(w + 1) * 512],
            op0=ALU.mult, op1=ALU.subtract), f"{tag}_r3", 1)
    P.wait("s", f"{tag}_r3")
    P.opi("s", lambda E: E.activation(out=c.t_row, in_=c.r_row,
                                      func=AF.Ln, bias=c.consts[0:1, 2:3]), f"{tag}_r4", 1)
    P.wait("s", f"{tag}_r4")
    P.opi("s", lambda E: E.activation(out=c.r_row, in_=c.t_row,
                                      func=AF.Exp, scale=c.consts[0:1, 4:5]), f"{tag}_r5", 1)
    P.wait("t", f"{tag}_r5")
    for w in range(2):
        P.opi("t", lambda E, w=w: E.matmul(
            s1b[w][:, :], c.ones_f[:, :],
            c.m_row[0:1, w * 512:(w + 1) * 512], start=True, stop=True), f"{tag}_bc", 1)
        P.opi("t", lambda E, w=w: E.matmul(
            s2b[w][:, :], c.ones_f[:, :],
            c.r_row[0:1, w * 512:(w + 1) * 512], start=True, stop=True), f"{tag}_bc", 1)
    P.wait("s", f"{tag}_bc")
    for w in range(2):
        P.opi("s", lambda E, w=w: E.copy(out=c.mBb[:, w * 512:(w + 1) * 512],
                                         in_=s1b[w][:, :]), f"{tag}_b2", 1)
        P.opi("s", lambda E, w=w: E.copy(out=c.rBf[:, w * 512:(w + 1) * 512],
                                         in_=s2b[w][:, :]), f"{tag}_b2", 1)
    lnd = f"{tag}_lnd"
    lns = f"{tag}_lns"
    P.wait("v", f"{tag}_b2")
    for dc in range(NCH):
        if dc > 0:
            P.wait("v", lnd, dc)
        P.opi("v", lambda E, dc=dc: E.tensor_sub(
            c.sq[:], c.xmir[:, dc * TL:(dc + 1) * TL], c.mBb[:]), lns, 1)
        P.wait("v", lns, dc + 1)
        P.opi("v", lambda E, dc=dc: E.tensor_mul(
            c.xmir[:, dc * TL:(dc + 1) * TL], c.sq[:], c.rBf[:]), lnd, 1)
    return lnd



def emit_proj_fm(c, tag, wT_dram, n_out_chunks, out_cb, gate_g=(),
                 in_sb=None, in_cols=TL, ln_sem=None, extra_chunk=None):
    P = c.P
    src = in_sb if in_sb is not None else c.xmir
    nw = in_cols // 512
    mm = f"{tag}_mm"
    gd = [f"{tag}_wl0", f"{tag}_wl1"]
    P.sem(f"{tag}_ev")
    rnd = 0
    for oc in range(n_out_chunks):
        buf = oc % 2
        if oc < 2:
            for sname, val in gate_g:
                P.wait("g", sname, val)
        if oc >= 2:
            P.wait("g", mm, (oc - 1) * nw)
        P.opi("g", lambda E, oc=oc, buf=buf: E.dma_start(
            out=c.wstrip[buf][:, 0:NCH * 128],
            in_=wT_dram[oc, :, :]), gd[buf], 16)
        P.wait("t", gd[buf])
        if ln_sem is not None:
            P.wait("t", ln_sem)
        for w in range(nw):
            ps = c.pp[(oc * nw + w) % 2]
            if rnd >= 2:
                P.wait("t", f"{tag}_ev", rnd - 1)
            rnd += 1
            for dc in range(NCH):
                last = (dc == NCH - 1) and (extra_chunk is None)
                fn = lambda E, w=w, dc=dc, buf=buf, ps=ps, last=last: E.matmul(
                    ps[:, :], c.wstrip[buf][:, dc * 128:(dc + 1) * 128],
                    src[:, dc * in_cols + w * 512: dc * in_cols + w * 512 + 512],
                    start=(dc == 0), stop=last)
                if last:
                    P.opi("t", fn, mm, 1)
                else:
                    P.op("t", fn)
            if extra_chunk is not None:
                P.opi("t", lambda E, oc=oc, w=w, ps=ps: extra_chunk(E, oc, w, ps), mm, 1)
            out_cb(oc, w, ps, P.cnt[mm])
    return mm


def make_evac_to_sbuf(c, tag, dst_sb, dst_col, bias_ap=None):
    P = c.P

    def cb(oc, w, ps, mmcnt):
        P.wait("s", f"{tag}_mm", mmcnt)
        col = dst_col(oc, w)
        if bias_ap is not None:
            P.opi("s", lambda E, ps=ps, col=col, oc=oc: E.activation(
                out=dst_sb[:, col:col + 512], in_=ps[:, :], func=AF.Identity,
                bias=bias_ap[:, oc:oc + 1]), f"{tag}_ev", 1)
        else:
            P.opi("s", lambda E, ps=ps, col=col: E.copy(
                out=dst_sb[:, col:col + 512], in_=ps[:, :]), f"{tag}_ev", 1)
    return cb


def make_evac_residual_store(c, tag, x_dram, xo_dram, bias_ap=None,
                             mir_sem=None, mir_gate=None):
    P = c.P
    state = {"r": 0}

    def cb(oc, w, ps, mmcnt):
        r = state["r"]
        state["r"] += 1
        par, ev = r % 2, r % 2
        xl = f"{tag}_xl{par}"
        st = f"{tag}_st{ev}"
        if r >= 2:
            P.wait("y", f"{tag}_res", r - 1)
        P.opi("y", lambda E, oc=oc, w=w, par=par: E.dma_start(
            out=c.xstg[par][:, :],
            in_=x_dram[oc * 128:(oc + 1) * 128, w * 512:(w + 1) * 512]), xl, 16)
        P.wait("s", f"{tag}_mm", mmcnt)
        if r >= 2:
            P.wait("s", f"{tag}_res", r - 1)
            P.wait("s", st, P.cnt.get(st, 0))
            if mir_sem is not None:
                P.wait("s", mir_sem, r - 1)
        if bias_ap is not None:
            P.opi("s", lambda E, ps=ps, oc=oc, ev=ev: E.activation(
                out=c.evac[ev][:, :], in_=ps[:, :], func=AF.Identity,
                bias=bias_ap[:, oc:oc + 1]), f"{tag}_ev", 1)
        else:
            P.opi("s", lambda E, ps=ps, ev=ev: E.copy(
                out=c.evac[ev][:, :], in_=ps[:, :]), f"{tag}_ev", 1)
        P.wait("v", f"{tag}_ev")
        P.wait("v", xl)
        P.opi("v", lambda E, ev=ev, par=par: E.tensor_add(
            c.evac[ev][:, :], c.evac[ev][:, :], c.xstg[par][:, :]), f"{tag}_res", 1)
        if mir_sem is not None:
            if r == 0 and mir_gate is not None:
                P.wait("v", *mir_gate)
            P.opi("v", lambda E, ev=ev, oc=oc, w=w: E.tensor_copy(
                c.xmir[:, oc * TL + w * 512: oc * TL + w * 512 + 512],
                c.evac[ev][:, :]), mir_sem, 1)
        P.wait("y", f"{tag}_res")
        P.opi("y", lambda E, oc=oc, w=w, ev=ev: E.dma_start(
            out=xo_dram[oc * 128:(oc + 1) * 128, w * 512:(w + 1) * 512],
            in_=c.evac[ev][:, :]), st, 16)
    return cb



def emit_l0a(c, io):
    nc, P = c.nc, c.P
    A = c.A
    kgb = A("l0_kgb", [128, 4096], BF16).ap()
    vgb = A("l0_vgb", [128, 32 * 128], BF16).ap()
    pb = A("l0_pb", [128, 32 * 256], BF16).ap()
    kcon = pb[:, 0:TL]
    vcon = pb[:, TL:TL + 8 * 128]
    mbuf = A("l0_mbuf", [128, N_MASKS_W * 256], BF16).ap()
    sel4 = A("l0_sel4", [128, 1], F32).ap()
    biasq = A("l0_biasq", [128, 32], F32).ap()
    biaso = A("l0_biaso", [128, 32], F32).ap()
    biask = A("l0_biask", [128, 1], F32).ap()
    kvrow = A("l0_kvrow", [1, 256], BF16).ap()

    P.opi("y", lambda E: E.dma_start(out=biasq[:], in_=io["wq_bT"][:, :]), "l0_sld", 16)
    P.opi("y", lambda E: E.dma_start(out=biaso[:], in_=io["wo_bT"][:, :]), "l0_sld", 16)
    P.opi("y", lambda E: E.dma_start(out=biask[:], in_=io["wk_bT"][:, :]), "l0_sld", 16)
    P.opi("y", lambda E: E.dma_start(out=kvrow[:], in_=io["kv_brow"][:, :]), "l0_sld", 16)
    P.opi("y", lambda E: E.dma_start(out=sel4[:, :], in_=io["sel4"][:, :]), "l0_sld", 16)
    P.wait("s", "l0_sld", 80)

    ln = emit_stats_ln(c, io["x"], "l0ln", hooks=getattr(c, "a1e_hooks", None))

    P.opi("g", lambda E: E.dma_start(
        out=c.wstrip[0][:, 0:NCH * 128], in_=io["wkT"][0, :, :]), "l0_gld", 16)
    P.opi("g", lambda E: E.dma_start(
        out=c.wstrip[1][:, 0:NCH * 128], in_=io["wvT"][0, :, :]), "l0_gld", 16)
    P.wait("t", "l0_gld", 32)
    P.wait("t", "l0_sld", 80)
    P.wait("t", ln)
    for w in range(2):
        for dc in range(NCH):
            fn = lambda E, w=w, dc=dc: E.matmul(
                c.ss[w][:, :], c.wstrip[0][:, dc * 128:(dc + 1) * 128],
                c.xmir[:, dc * TL + w * 512: dc * TL + w * 512 + 512],
                start=(dc == 0), stop=(dc == NCH - 1))
            if dc == NCH - 1:
                P.opi("t", fn, "l0_kmm", 1)
            else:
                P.op("t", fn)
    for w in range(2):
        P.wait("s", "l0_kmm", w + 1)
        P.opi("s", lambda E, w=w: E.activation(
            out=kcon[:, w * 512:(w + 1) * 512], in_=c.ss[w][:, :],
            func=AF.Identity, bias=biask[:, 0:1]), "l0_kev", 1)
    for tt in range(8):
        ps = c.pp[tt % 2]
        if tt >= 2:
            P.wait("t", "l0_vev", tt - 1)
        for dc in range(NCH):
            P.op("t", lambda E, tt=tt, dc=dc, ps=ps: E.matmul(
                ps[:, 0:128],
                c.xmir[:, dc * TL + tt * 128: dc * TL + tt * 128 + 128],
                c.wstrip[1][:, dc * 128:(dc + 1) * 128],
                start=(dc == 0), stop=False))
        P.opi("t", lambda E, ps=ps: E.matmul(
            ps[:, 0:128], c.ones_br[:, :], kvrow[0:1, 128:256],
            start=False, stop=True), "l0_vmm", 1)
        P.wait("s", "l0_vmm")
        P.opi("s", lambda E, tt=tt, ps=ps: E.copy(
            out=vcon[:, tt * 128:(tt + 1) * 128], in_=ps[:, 0:128]), "l0_vev", 1)

    P.wait("y", "l0_kev", 2)
    P.opi("y", lambda E: E.dma_start(out=io["kcon_d"][:, :], in_=kcon[:, :]), "l0_cst", 16)
    P.wait("y", "l0_vev", 8)
    P.opi("y", lambda E: E.dma_start(
        out=io["vcon_d"].rearrange("(t p) o -> p t o", p=128),
        in_=vcon.rearrange("p (t o) -> p t o", o=128)), "l0_cst", 16)
    P.wait("g", "l0_cst", 32)
    P.opi("g", lambda E: E.collective_compute(
        "AllGather", ALU.bypass, replica_groups=[[0, 1, 2, 3], [4, 5, 6, 7]],
        ins=[io["kcon_d"].opt()], outs=[io["kg_d"].opt()]), "l0_cc", 1)
    P.opi("g", lambda E: E.collective_compute(
        "AllGather", ALU.bypass, replica_groups=[[0, 1, 2, 3], [4, 5, 6, 7]],
        ins=[io["vcon_d"].opt()], outs=[io["vg_d"].opt()]), "l0_cc", 1)

    P.wait("t", "l0_vev", 8)
    if getattr(c, "has_a1e", False):
        P.wait("s", "g:a1vst", 16)
    qcb = make_evac_to_sbuf(c, "l0q", c.qb, lambda oc, w: oc * TL + w * 512,
                            bias_ap=biasq)
    emit_proj_fm(c, "l0q", io["wqT"], 32, qcb,
                 gate_g=(("l0_kmm", 2), ("l0_vmm", 8)), ln_sem=ln)

    P.wait("y", "l0_cc", 2)
    P.wait("y", "l0_cst", 32)
    for r in range(4):
        P.opi("y", lambda E, r=r: E.dma_start(
            out=kgb[:, r * TL:(r + 1) * TL],
            in_=io["kg_d"][r * 128:(r + 1) * 128, :]), "l0_kvg", 16)
        P.opi("y", lambda E, r=r: E.dma_start(
            out=vgb[:, r * 8 * 128:(r + 1) * 8 * 128].rearrange("p (t o) -> p t o", o=128),
            in_=io["vg_d"][r * TL:(r + 1) * TL, :].rearrange("(t p) o -> p t o", p=128)),
            "l0_kvg", 16)

    P.wait("t", "l0_kvg", 16 * 8)
    P.wait("t", "l0q_ev", 64)
    P.wait("t", "l0_kev", 2)
    P.wait("s", "l0_cst", 32)
    avs, exs, nms = "l0_av", "l0_ex", "l0_nm"
    for s in (avs, exs, nms):
        P.sem(s)
    ridx = 0
    pend = [None]
    prev_ngrp = [None]
    for w in range(4):
        chunks = S_W[w]
        nchk = len(chunks)
        P.wait("y", "l0_msk", P.cnt.get("l0_msk", 0))
        P.opi("y", lambda E, w=w: E.dma_start(
            out=mbuf.rearrange("p (m q) -> p m q", q=256),
            in_=io["masks0"][w * 8:(w + 1) * 8, :, :].rearrange("m p q -> p m q")),
            "l0_mld", 16)
        P.wait("v", "l0_mld")
        for h in range(32):
            qcol = h * TL + w * 256
            pav = c.pp[ridx % 2]
            pdn = c.pd[ridx % 2]
            P.wait("s", avs, P.cnt[avs])
            P.wait("t", nms, P.cnt[nms])

            def l0_av(i, j, last):
                if (w, j) in MASK_SLOT:
                    P.wait("t", "l0_msk", mskcnt[i])
                else:
                    P.wait("t", exs, excnt[i])
                fn = lambda E, i=i, j=j, pav=pav, last=last: E.matmul(
                    pav[:, 0:256], vgb[:, j * 128:(j + 1) * 128],
                    pb[:, i * 256:(i + 1) * 256], start=(i == 0), stop=last)
                if last:
                    P.opi("t", fn, avs, 1)
                else:
                    P.op("t", fn)

            nb = nchk // 4
            planned_dnb = set()

            def l0_dnb(j, pdn=pdn, nb=nb):
                planned_dnb.add(j)
                if j == 0:
                    P.wait("t", "l0_nbc", P.cnt.get("l0_nbc", 0))
                for u in range(4):
                    i = 4 * j + u
                    fn = lambda E, i=i, u=u, pdn=pdn, j=j, nb=nb: E.matmul(
                        pdn[32 * u:32 * u + 1, 0:256], c.ones_b[:, :],
                        pb[:, i * 256:(i + 1) * 256],
                        start=(j == 0), stop=(j == nb - 1),
                        tile_position=(0, 32 * u))
                    if j == nb - 1 and u == 3:
                        P.opi("t", fn, "l0_dnf", 1)
                    else:
                        P.op("t", fn)

            excnt = {}
            mskcnt = {}
            ngrp = nchk // 2
            for g in range(ngrp):
                ii = (2 * g, 2 * g + 1)
                bank = g % 4
                P.wait("t", exs, max(0, P.cnt[exs] - 3))
                for u, i in enumerate(ii):
                    j = chunks[i]
                    fn = lambda E, j=j, bank=bank, u=u, qcol=qcol: E.matmul(
                        c.ssb[:, bank * 512 + u * 256: bank * 512 + u * 256 + 256],
                        kgb[:, j * 128:(j + 1) * 128],
                        c.qb[:, qcol:qcol + 256], start=True, stop=True)
                    if u == 1:
                        P.opi("t", fn, "l0_smm", 1)
                    else:
                        P.op("t", fn)
                if prev_ngrp[0] is not None and g == max(0, prev_ngrp[0] - 2):
                    P.wait("s", "l0_dnf", P.cnt.get("l0_dnf", 0))
                P.wait("s", "l0_smm")
                ec = P.opi("s", lambda E, g=g, bank=bank: E.activation(
                    out=pb[:, 2 * g * 256: 2 * g * 256 + 512],
                    in_=c.ssb[:, bank * 512: bank * 512 + 512],
                    func=AF.Exp, bias=c.consts[:, 1:2], scale=c.consts[:, 0:1]), exs, 1)
                for i in ii:
                    excnt[i] = ec
                    j = chunks[i]
                    if (w, j) in MASK_SLOT:
                        ms = MASK_SLOT[(w, j)]
                        P.wait("v", exs)
                        mskcnt[i] = P.opi("v", lambda E, i=i, ms=ms: E.tensor_mul(
                            pb[:, i * 256:(i + 1) * 256], pb[:, i * 256:(i + 1) * 256],
                            mbuf[:, ms * 256:(ms + 1) * 256]), "l0_msk", 1)
                if g == 1 and pend[0] is not None:
                    pend[0]()
                    pend[0] = None
                if g >= 1:
                    l0_av(2 * (g - 1), chunks[2 * (g - 1)], False)
                    l0_av(2 * g - 1, chunks[2 * g - 1], False)
                    if g >= 2 and g % 2 == 0:
                        l0_dnb((g - 2) // 2)
            l0_av(nchk - 2, chunks[nchk - 2], False)
            l0_av(nchk - 1, chunks[nchk - 1], True)
            for j in range(nb):
                if j not in planned_dnb:
                    l0_dnb(j)
            avc = P.cnt[avs]
            dnfc = P.cnt["l0_dnf"]

            def mk_norm0(pav=pav, pdn=pdn, qcol=qcol, avc=avc, dnfc=dnfc):
                def plan():
                    P.wait("v", avs, avc)
                    P.wait("v", "l0_dnf", dnfc)
                    P.wait("v", nms, P.cnt[nms])
                    P.opi("v", lambda E: E.tensor_copy(
                        c.rBf[:, 0:256], pdn[:, 0:256]), "l0_dne", 1)
                    P.wait("t", "l0_dne")
                    P.opi("t", lambda E: E.matmul(
                        pdn[0:1, 0:256], sel4[:, :], c.rBf[:, 0:256],
                        start=True, stop=True), "l0_dsm", 1)
                    P.wait("v", "l0_dsm")
                    P.opi("v", lambda E: E.reciprocal(
                        out=c.rBf[0:1, 256:512], in_=pdn[0:1, 0:256]), "l0_rec", 1)
                    P.wait("t", "l0_rec")
                    P.opi("t", lambda E: E.matmul(
                        pdn[:, 256:512], c.ones_f[:, :], c.rBf[0:1, 256:512],
                        start=True, stop=True), "l0_bcm", 1)
                    P.wait("v", "l0_bcm")
                    P.opi("v", lambda E: E.tensor_copy(
                        c.rBf[:, 512:768], pdn[:, 256:512]), "l0_nbc", 1)
                    P.opi("v", lambda E: E.tensor_mul(
                        c.qb[:, qcol:qcol + 256], pav[:, 0:256],
                        c.rBf[:, 512:768]), nms, 1)
                return plan

            pend[0] = mk_norm0()
            prev_ngrp[0] = ngrp
            ridx += 1
    if pend[0] is not None:
        pend[0]()
        pend[0] = None

    P.wait("t", nms)
    P.wait("s", nms)
    fused = getattr(c, "has_a1e", False)
    ocb = make_evac_residual_store(c, "l0o", io["x"], io["xo"], bias_ap=biaso,
                                   mir_sem="g:mir0" if fused else None,
                                   mir_gate=("l0q_mm", 64))
    emit_proj_fm(c, "l0o", io["woT"], 32, ocb, gate_g=(("l0q_mm", 64),),
                 in_sb=c.qb)
    P.wait("y", "l0o_st0", P.cnt["l0o_st0"])
    P.wait("y", "l0o_st1", P.cnt["l0o_st1"])


def build_l0a():
    nc = bass.Bass("TRN2", target_bir_lowering=False)
    io = {}
    def din(name, shape, dt=F32):
        io[name] = nc.dram_tensor(name, shape, dt, kind="ExternalInput").ap()
    din("x", [D, TL])
    din("wqT", [32, 128, D]); din("wkT", [1, 128, D]); din("wvT", [1, 128, D]); din("woT", [32, 128, D])
    din("wq_bT", [128, 32]); din("wo_bT", [128, 32]); din("wk_bT", [128, 1])
    din("kv_brow", [1, 256], BF16)
    din("masks0", [32, 128, 256], BF16)
    din("sel4", [128, 1])
    io["xo"] = nc.dram_tensor("xo", [D, TL], F32, kind="ExternalOutput").ap()
    io["kcon_d"] = nc.dram_tensor("kcon_d", [128, TL], BF16).ap()
    io["vcon_d"] = nc.dram_tensor("vcon_d", [TL, 128], BF16).ap()
    io["kg_d"] = nc.dram_tensor("kg_d", [512, TL], BF16).ap()
    io["vg_d"] = nc.dram_tensor("vg_d", [4096, 128], BF16).ap()
    P = Prog(nc)
    c = alloc_common(nc, P)
    emit_l0a(c, io)
    P.finalize()
    return nc



def emit_l0f(c, io):
    nc, P = c.nc, c.P
    A = c.A
    wrt = A("f0_wrt", [128, NCH * 8], BF16).ap()
    bstrip = A("f0_bstrip", [128, NCH * 128], BF16).ap()
    hb = A("f0_hb", [128, TL], BF16).ap()
    cB = A("f0_cB", [128, TL], BF16).ap()
    ctm = A("f0_ctm", [128, 8], BF16).ap()
    ctT = A("f0_ctT", [8, TL], BF16).ap()
    Esb = A("f0_E", [8, 128], BF16).ap()
    idn = A("f0_idn", [128, 128], BF16).ap()
    rsc = A("f0_rsc", [128, 32], F32).ap()
    zt = A("f0_zt", [128, 8], F32).ap()
    wzt = A("f0_wzt", [128, 8], F32).ap()
    biasb = A("f0_biasb", [128, 32], F32).ap()
    pdb = c.pd[0].bitcast(BF16)

    P.opi("y", lambda E: E.dma_start(out=biasb[:], in_=io["b_baseT"][:, :]), "f0_sld", 16)
    P.opi("y", lambda E: E.dma_start(out=Esb[:], in_=io["Econst"][:, :]), "f0_sld", 16)
    P.opi("y", lambda E: E.dma_start(out=idn[:], in_=io["ident"][:, :]), "f0_sld", 16)
    P.opi("g", lambda E: E.dma_start(
        out=wrt.rearrange("p (k e) -> p k e", e=8),
        in_=io["wrT"].rearrange("(k p) e -> p k e", p=128)), "f0_gld", 16)
    P.opi("g", lambda E: E.dma_start(out=bstrip[:, :], in_=io["BfT"][:, :]), "f0_gld", 16)
    P.opi("g", lambda E: E.dma_start(
        out=c.wstrip[0][:, 0:NCH * 128], in_=io["AT"][0, :, :]), "f0_gld", 16)

    ln = emit_stats_ln(c, io["x"], "f0ln",
                       pre_sem="g:mir0" if getattr(c, "has_a1e", False) else None)

    P.wait("t", ln)
    P.wait("t", "f0_sld", 48)
    P.wait("t", "f0_gld", 48)
    for qt in range(8):
        ps = c.ss[qt % 2]
        if qt >= 2:
            P.wait("t", "f0_ct", qt - 1)
        for dc in range(NCH):
            fn = lambda E, qt=qt, dc=dc, ps=ps: E.matmul(
                ps[:, 0:8], c.xmir[:, dc * TL + qt * 128: dc * TL + qt * 128 + 128],
                wrt[:, dc * 8:(dc + 1) * 8],
                start=(dc == 0), stop=(dc == NCH - 1))
            if dc == NCH - 1:
                P.opi("t", fn, "f0_rmm", 1)
            else:
                P.op("t", fn)
        P.wait("v", "f0_rmm")
        if qt > 0:
            P.wait("v", "f0_ctv", qt)
        P.opi("v", lambda E, ps=ps: E.reduce_max(
            rsc[:, 0:1], ps[:, 0:8], axis=mybir.AxisListType.X), "f0_t1", 1)
        P.wait("v", "f0_t1")
        P.opi("v", lambda E: E.tensor_scalar(
            rsc[:, 1:2], rsc[:, 0:1], -1.0, None, ALU.mult), "f0_t2", 1)
        P.wait("s", "f0_t2")
        P.opi("s", lambda E, ps=ps: E.activation(
            out=zt[:], in_=ps[:, 0:8], func=AF.Exp,
            bias=rsc[:, 1:2]), "f0_t3", 1)
        P.wait("v", "f0_t2")
        P.opi("v", lambda E, ps=ps: E.tensor_scalar(
            rsc[:, 8:16], ps[:, 0:8], rsc[:, 0:1], None, ALU.is_ge), "f0_t4", 1)
        P.wait("v", "f0_t4")
        P.opi("v", lambda E, ps=ps: E.scalar_tensor_tensor(
            out=rsc[:, 16:24], in0=rsc[:, 8:16], scalar=-1e30, in1=ps[:, 0:8],
            op0=ALU.mult, op1=ALU.add), "f0_t5", 1)
        P.wait("v", "f0_t5")
        P.opi("v", lambda E: E.reduce_max(
            rsc[:, 4:5], rsc[:, 16:24], axis=mybir.AxisListType.X), "f0_t6", 1)
        P.wait("v", "f0_t6")
        P.opi("v", lambda E, ps=ps: E.tensor_scalar(
            rsc[:, 8:16], ps[:, 0:8], rsc[:, 4:5], None, ALU.is_ge), "f0_t7", 1)
        P.wait("v", "f0_t7")
        P.wait("v", "f0_t3")
        P.opi("v", lambda E: E.tensor_mul(wzt[:], zt[:], rsc[:, 8:16]), "f0_t8", 1)
        P.wait("v", "f0_t8")
        P.opi("v", lambda E: E.reduce_sum(
            rsc[:, 2:3], wzt[:], axis=mybir.AxisListType.X), "f0_t9", 1)
        P.wait("v", "f0_t9")
        P.opi("v", lambda E: E.reciprocal(rsc[:, 3:4], rsc[:, 2:3]), "f0_t10", 1)
        P.wait("v", "f0_t10")
        P.opi("v", lambda E: E.tensor_scalar(
            ctm[:], wzt[:], rsc[:, 3:4], 2.0, ALU.mult, ALU.mult), "f0_ctv", 1)
        P.wait("t", "f0_ctv")
        if qt > 0:
            P.wait("t", "f0_cte", qt)
        P.opi("t", lambda E, qt=qt: E.transpose(
            pdb[0:8, qt * 128:(qt + 1) * 128], ctm[:], idn[:]), "f0_ct", 1)
        P.wait("s", "f0_ct")
        P.opi("s", lambda E, qt=qt: E.copy(
            out=ctT[:, qt * 128:(qt + 1) * 128],
            in_=pdb[0:8, qt * 128:(qt + 1) * 128]), "f0_cte", 1)
    P.wait("t", "f0_cte", 8)
    for w in range(2):
        P.opi("t", lambda E, w=w: E.matmul(
            c.pt[w][:, :], Esb[:, :], ctT[:, w * 512:(w + 1) * 512],
            start=True, stop=True), "f0_ex", 1)
        P.wait("s", "f0_ex")
        P.opi("s", lambda E, w=w: E.copy(
            out=cB[:, w * 512:(w + 1) * 512], in_=c.pt[w][:, :]), "f0_exe", 1)

    for w in range(2):
        ps = c.pp[w]
        for dc in range(NCH):
            fn = lambda E, w=w, dc=dc, ps=ps: E.matmul(
                ps[:, :], c.wstrip[0][:, dc * 128:(dc + 1) * 128],
                c.xmir[:, dc * TL + w * 512: dc * TL + w * 512 + 512],
                start=(dc == 0), stop=(dc == NCH - 1))
            if dc == NCH - 1:
                P.opi("t", fn, "f0_hmm", 1)
            else:
                P.op("t", fn)
        P.wait("s", "f0_hmm")
        P.opi("s", lambda E, w=w, ps=ps: E.copy(
            out=hb[:, w * 512:(w + 1) * 512], in_=ps[:, :]), "f0_hev", 1)
    P.wait("v", "f0_hev", 2)
    P.wait("v", "f0_exe", 2)
    for w in range(2):
        P.opi("v", lambda E, w=w: E.tensor_mul(
            hb[:, w * 512:(w + 1) * 512], hb[:, w * 512:(w + 1) * 512],
            cB[:, w * 512:(w + 1) * 512]), "f0_hw", 1)

    P.wait("t", "f0_hw", 2)
    P.wait("t", "f0_hev", 2)

    def extra(E, oc, w, ps):
        return E.matmul(ps[:, :], bstrip[:, oc * 128:(oc + 1) * 128],
                        hb[:, w * 512:(w + 1) * 512], start=False, stop=True)

    P.wait("s", "f0_sld", 48)
    ocb = make_evac_residual_store(c, "f0o", io["x"], io["xo"], bias_ap=biasb)
    emit_proj_fm(c, "f0o", io["wbT"], 32, ocb, gate_g=(("f0_hmm", 2),),
                 extra_chunk=extra)
    P.wait("y", "f0o_st0", P.cnt["f0o_st0"])
    P.wait("y", "f0o_st1", P.cnt["f0o_st1"])


def build_l0f():
    nc = bass.Bass("TRN2", target_bir_lowering=False)
    io = {}
    def din(name, shape, dt=F32):
        io[name] = nc.dram_tensor(name, shape, dt, kind="ExternalInput").ap()
    din("x", [D, TL])
    din("wbT", [32, 128, D]); din("wrT", [D, 8]); din("AT", [1, 128, D]); din("BfT", [128, D])
    din("b_baseT", [128, 32])
    din("Econst", [8, 128], BF16)
    din("ident", [128, 128], BF16)
    io["xo"] = nc.dram_tensor("xo", [D, TL], F32, kind="ExternalOutput").ap()
    P = Prog(nc)
    c = alloc_common(nc, P)
    emit_l0f(c, io)
    P.finalize()
    return nc



def l1_k_loads(kvh):
    out = []
    for r in range(8):
        out.append(("kgc_d", 512 * r, r * 1024 + kvh * 128, 0, 512))
    for r in range(4):
        out.append(("kgn_d", 4096 + 1024 * r, r * 1024 + kvh * 128, 0, 1024))
    return out


def l1_v_loads(kvh):
    out = []
    for r in range(8):
        out.append(("vgc_d", 512 * r, r * 512, 512))
    for r in range(4):
        out.append(("vgn_d", 4096 + 1024 * r, r * 1024, 1024))
    return out



def emit_a1_head(c, io):
    nc, P = c.nc, c.P
    qb_addr = nc.lookup_mloc("qb").addr
    gK = nc.alloc_sbuf_tensor_at("a1e_gK", [128, 4096], F32, offset=qb_addr + 16384).ap()
    ab = nc.lookup_mloc("arena").addr
    c.a1e_vstg = nc.alloc_sbuf_tensor_at("a1e_vstg", [128, TL], BF16, offset=ab + 40000).ap()
    c.a1e_ktr = nc.alloc_sbuf_tensor_at("a1e_ktr", [128, TL], BF16, offset=ab + 42048).ap()
    c.a1e_idn = nc.alloc_sbuf_tensor_at("a1e_idn", [128, 128], BF16, offset=ab + 44096).ap()
    eidx = nc.alloc_sbuf_tensor_at("a1e_eidx", [128, 2], I32, offset=ab + 44352).ap()
    selc = nc.alloc_sbuf_tensor_at("a1e_selc", [128, 2], I32, offset=ab + 44384).ap()

    P.opi("y", lambda E: E.dma_start(out=selc[:, :], in_=io["selconst"][:, :]), "g:a1sld", 16)
    P.opi("y", lambda E: E.dma_start(out=c.a1e_idn[:], in_=io["ident"][:, :]), "g:a1sld", 16)
    P.wait("g", "g:a1sld", 16)
    P.opi("g", lambda E: E.indirect_dma_start(
        out=eidx[:, 0:1], out_offset=None,
        in_=io["blockidx"][0:2, :],
        in_offset=bass.IndirectOffsetOnAxis(selc[:, 0:1], 0)), "g:a1gi", 16)
    P.wait("v", "g:a1gi", 16)
    for dbl in range(6):
        src_col, dst_col = (0, 1) if dbl % 2 == 0 else (1, 0)
        P.opi("v", lambda E, s=src_col, d=dst_col: E.tensor_add(
            eidx[:, d:d + 1], eidx[:, s:s + 1], eidx[:, s:s + 1]), "g:a1ei", 1)
        P.wait("v", "g:a1ei", dbl + 1)
    P.opi("v", lambda E: E.tensor_add(eidx[:, 1:2], eidx[:, 0:1], selc[:, 1:2]), "g:a1ei", 1)
    P.wait("v", "g:a1ei", 7)
    P.opi("v", lambda E: E.tensor_copy(eidx[:, 0:1], eidx[:, 1:2]), "g:a1ei", 1)
    P.wait("g", "g:a1ei", 8)
    P.opi("g", lambda E: E.indirect_dma_start(
        out=gK[:, :], out_offset=None,
        in_=io["kcache"][:, :],
        in_offset=bass.IndirectOffsetOnAxis(eidx[:, 0:1], 0)), "g:a1gk", 16)

    def h_restage():
        P.wait("g", "g:a1gk", 16)
        P.opi("g", lambda E: E.dma_start(
            out=io["kstage"].rearrange("(p j) c -> p j c", j=4),
            in_=gK.rearrange("p (j c) -> p j c", c=1024)), "g:a1gk", 16)

    def h_gatherv():
        P.wait("g", "g:a1gk", 32)
        P.opi("g", lambda E: E.indirect_dma_start(
            out=gK[:, :], out_offset=None,
            in_=io["vcache"][:, :],
            in_offset=bass.IndirectOffsetOnAxis(eidx[:, 0:1], 0)), "g:a1gv", 16)

    def h_storev():
        P.wait("g", "g:a1gv", 16)
        P.opi("g", lambda E: E.dma_start(
            out=io["vcach_d"].rearrange("(p j) c -> p j c", j=4),
            in_=gK.rearrange("p (j c) -> p j c", c=1024)), "g:a1vst", 16)

    c.a1e_hooks = {8: h_restage, 16: h_gatherv, 24: h_storev}


def emit_a1_tail(c, io):
    nc, P = c.nc, c.P
    vstg, ktr, idn = c.a1e_vstg, c.a1e_ktr, c.a1e_idn
    for tt in range(4):
        P.wait("y", "g:a1gk", 32)
        if tt > 0:
            P.wait("y", "g:a1kt", 8 * tt)
        P.opi("y", lambda E, tt=tt: E.dma_start(
            out=vstg.rearrange("p (kv h) -> p kv h", h=128),
            in_=io["kstage"][tt * 128:(tt + 1) * 128, :].rearrange(
                "p (kv h) -> p kv h", h=128)), "g:a1ks", 16)
        P.wait("t", "g:a1ks")
        if tt > 0:
            P.wait("t", "g:a1kte", 8 * tt)
            P.wait("s", "g:a1kst", 16 * tt)
        for kv in range(8):
            if tt * 8 + kv >= 2:
                P.wait("t", "g:a1kte", tt * 8 + kv - 1)
            P.opi("t", lambda E, kv=kv: E.transpose(
                c.pd[kv % 2].bitcast(BF16)[:, (kv // 2) * 128:(kv // 2) * 128 + 128],
                vstg[:, kv * 128:(kv + 1) * 128], idn[:]), "g:a1kt", 1)
            P.wait("s", "g:a1kt")
            P.opi("s", lambda E, kv=kv: E.copy(
                out=ktr[:, kv * 128:(kv + 1) * 128],
                in_=c.pd[kv % 2].bitcast(BF16)[:, (kv // 2) * 128:(kv // 2) * 128 + 128]),
                "g:a1kte", 1)
        P.wait("y", "g:a1kte")
        P.opi("y", lambda E, tt=tt: E.dma_start(
            out=io["kcach_d"][:, tt * 128:(tt + 1) * 128].rearrange(
                "(kv h) t -> h kv t", h=128),
            in_=ktr.rearrange("h (kv t) -> h kv t", t=128)), "g:a1kst", 16)
    P.wait("g", "g:a1kst", 64)
    P.wait("g", "g:a1vst", 16)
    P.opi("g", lambda E: E.collective_compute(
        "AllGather", ALU.bypass, replica_groups=[[0, 1, 2, 3, 4, 5, 6, 7]],
        ins=[io["kcach_d"].opt()], outs=[io["kgc_d"].opt()]), "g:a1ecc", 1)
    P.opi("g", lambda E: E.collective_compute(
        "AllGather", ALU.bypass, replica_groups=[[0, 1, 2, 3, 4, 5, 6, 7]],
        ins=[io["vcach_d"].opt()], outs=[io["vgc_d"].opt()]), "g:a1ecc", 1)


def emit_l1a(c, io):
    import os
    CUT = int(os.environ.get("L1A_CUT", "99"))
    nc, P = c.nc, c.P
    A = c.A
    kkvh = A("a1_kkv", [128, 8192], BF16)
    vkvh = A("a1_vkv", [128, 8192], BF16)
    kkv = kkvh.ap()
    vkv = vkvh.ap()
    pbh = A("a1_pb", [128, 8 * 512], BF16)
    pb = pbh.ap()
    rtb = A("a1_rt", [128, 128], BF16).ap()
    idn = A("a1_idn", [128, 128], BF16).ap()
    pb_base = nc.lookup_mloc(pbh).addr
    vstg = nc.alloc_sbuf_tensor_at("a1_vstg", [128, TL], BF16, offset=pb_base).ap()
    ktr = nc.alloc_sbuf_tensor_at("a1_ktr", [128, TL], BF16, offset=pb_base + 2048).ap()
    qtmp = [nc.alloc_sbuf_tensor_at(f"a1_qt{i}", [128, 512], BF16,
                                    offset=pb_base + 4096 + i * 1024).ap()
            for i in range(2)]
    sel4 = A("a1_sel4", [128, 1], F32).ap()
    rsc = A("a1_rsc", [128, 256], F32).ap()
    ropecs = A("a1_ropecs", [128, 2], F32).ap()

    vfm = nc.alloc_sbuf_tensor_at("a1_vfm", [128, 8192], BF16,
                                  offset=nc.lookup_mloc(kkvh).addr).ap()
    kfm = nc.alloc_sbuf_tensor_at("a1_kfm", [128, 8192], BF16,
                                  offset=nc.lookup_mloc(vkvh).addr).ap()

    P.opi("y", lambda E: E.dma_start(out=idn[:], in_=io["ident"][:, :]), "a1_sld", 16)
    P.opi("y", lambda E: E.dma_start(out=ropecs[:, :], in_=io["ropecs"][:, :]), "a1_sld", 16)
    P.opi("y", lambda E: E.dma_start(out=sel4[:, :], in_=io["sel4"][:, :]), "a1_sld", 16)

    if CUT <= 1:
        return
    ln = emit_stats_ln(c, io["x"], "a1ln")

    P.opi("y", lambda E: E.dma_start(out=rtb[:], in_=io["m1c"][:, :]), "a1_rld", 16)
    P.opi("y", lambda E: E.dma_start(out=ktr[:, 0:128], in_=io["m2c"][:, :]), "a1_rld", 16)
    P.wait("v", "a1_rld", 32)
    P.wait("v", "a1_sld", 48)
    P.opi("v", lambda E: E.tensor_scalar(
        rsc[:, 0:128], rtb[:, :], ropecs[:, 0:1], None, ALU.mult), "a1_r1", 1)
    P.wait("v", "a1_r1")
    P.opi("v", lambda E: E.scalar_tensor_tensor(
        out=rsc[:, 128:256], in0=ktr[:, 0:128], scalar=ropecs[:, 1:2],
        in1=rsc[:, 0:128], op0=ALU.mult, op1=ALU.add), "a1_r2", 1)
    P.wait("v", "a1_r2")
    P.opi("v", lambda E: E.tensor_copy(rtb[:, :], rsc[:, 128:256]), "a1_rt", 1)

    if CUT <= 2:
        return
    vcb = make_evac_to_sbuf(c, "a1v", vfm, lambda oc, w: oc * TL + w * 512)
    emit_proj_fm(c, "a1v", io["wvT"], 8, vcb, ln_sem=ln)
    P.wait("t", "a1v_ev", 16)
    P.wait("t", "a1_sld", 48)
    for tt in range(8):
        if tt > 0:
            P.wait("t", "a1_vte", 8 * tt)
            P.wait("s", "a1_vst", 16 * tt)
        for oc in range(8):
            if tt * 8 + oc >= 2:
                P.wait("t", "a1_vte", tt * 8 + oc - 1)
            P.opi("t", lambda E, tt=tt, oc=oc: E.transpose(
                c.pd[oc % 2].bitcast(BF16)[:, (oc // 2) * 128:(oc // 2) * 128 + 128],
                vfm[:, oc * TL + tt * 128: oc * TL + tt * 128 + 128], idn[:]), "a1_vt", 1)
            P.wait("s", "a1_vt")
            P.opi("s", lambda E, oc=oc: E.copy(
                out=vstg[:, oc * 128:(oc + 1) * 128],
                in_=c.pd[oc % 2].bitcast(BF16)[:, (oc // 2) * 128:(oc // 2) * 128 + 128]),
                "a1_vte", 1)
        P.wait("y", "a1_vte")
        P.opi("y", lambda E, tt=tt: E.dma_start(
            out=io["vnew_d"][tt * 128:(tt + 1) * 128, :],
            in_=vstg[:, :]), "a1_vst", 16)
    P.wait("t", "a1_rt")

    def kcb(oc, w, ps, mmcnt):
        P.wait("s", "a1k_mm", mmcnt)
        qi = (oc * 2 + w) % 2
        if oc * 2 + w >= 2:
            P.wait("s", "a1_krm", oc * 2 + w - 1)
        P.opi("s", lambda E, ps=ps, qi=qi: E.copy(
            out=qtmp[qi][:, :], in_=ps[:, :]), "a1k_qc", 1)
        P.wait("t", "a1k_qc")
        P.opi("t", lambda E, ps=ps, qi=qi: E.matmul(
            ps[:, :], rtb[:, :], qtmp[qi][:, :], start=True, stop=True), "a1_krm", 1)
        P.wait("s", "a1_krm")
        P.opi("s", lambda E, ps=ps, oc=oc, w=w: E.copy(
            out=kfm[:, oc * TL + w * 512: oc * TL + w * 512 + 512],
            in_=ps[:, :]), "a1k_ev", 1)

    emit_proj_fm(c, "a1k", io["wkT"], 8, kcb, gate_g=(("a1v_mm", 16),), ln_sem=ln)
    P.wait("y", "a1k_ev", 16)
    for kv in range(8):
        P.opi("y", lambda E, kv=kv: E.dma_start(
            out=io["knew_d"][kv * 128:(kv + 1) * 128, :],
            in_=kfm[:, kv * TL:(kv + 1) * TL]), "a1_knst", 16)

    if CUT <= 3:
        return
    P.wait("g", "a1_vst", 16 * 8)
    P.wait("g", "a1_knst", 16 * 8)
    P.opi("g", lambda E: E.collective_compute(
        "AllGather", ALU.bypass, replica_groups=[[0, 1, 2, 3, 4, 5, 6, 7]],
        ins=[io["knew_d"].opt()], outs=[io["kgn_d"].opt()]), "a1_cc", 1)
    P.opi("g", lambda E: E.collective_compute(
        "AllGather", ALU.bypass, replica_groups=[[0, 1, 2, 3, 4, 5, 6, 7]],
        ins=[io["vnew_d"].opt()], outs=[io["vgn_d"].opt()]), "a1_cc", 1)

    if CUT <= 4:
        return
    def qcb(oc, w, ps, mmcnt):
        P.wait("s", "a1q_mm", mmcnt)
        r = oc * 2 + w
        qi = r % 2
        if r >= 2:
            P.wait("s", "a1_qrm", r - 1)
        else:
            P.wait("s", "a1_krm", 16)
        P.opi("s", lambda E, ps=ps, qi=qi: E.copy(
            out=qtmp[qi][:, :], in_=ps[:, :]), "a1q_qc", 1)
        P.wait("t", "a1q_qc")
        P.opi("t", lambda E, ps=ps, qi=qi: E.matmul(
            ps[:, :], rtb[:, :], qtmp[qi][:, :], start=True, stop=True), "a1_qrm", 1)
        P.wait("s", "a1_qrm")
        P.opi("s", lambda E, ps=ps, oc=oc, w=w: E.copy(
            out=c.qb[:, oc * TL + w * 512: oc * TL + w * 512 + 512],
            in_=ps[:, :]), "a1q_ev", 1)

    emit_proj_fm(c, "a1q", io["wqT"], 32, qcb, gate_g=(("a1k_mm", 16),), ln_sem=ln)

    if CUT <= 5:
        return
    P.wait("y", "a1_cc", 2)
    P.wait("y", "g:a1ecc", 2)
    avs, exs, nms = "a1_av", "a1_ex", "a1_nm"
    for s in (avs, exs, nms):
        P.sem(s)
    P.sem("a1_kvl")
    ridx = 0
    pend = [None]
    for kvh in range(8):
        if ridx > 0:
            P.wait("y", avs, P.cnt[avs])
            P.wait("y", "a1_smm", P.cnt["a1_smm"])
        for (tn, kc, row, col, n) in l1_k_loads(kvh):
            P.opi("y", lambda E, tn=tn, kc=kc, row=row, col=col, n=n: E.dma_start(
                out=kkv[:, kc:kc + n],
                in_=io[tn][row:row + 128, col:col + n]), "a1_kvl", 16)
        for (tn, vc, row, n) in l1_v_loads(kvh):
            P.opi("y", lambda E, tn=tn, vc=vc, row=row, n=n, kvh=kvh: E.dma_start(
                out=vkv[:, vc:vc + n].rearrange("p (t h) -> p t h", h=128),
                in_=io[tn][row:row + n, kvh * 128:(kvh + 1) * 128].rearrange(
                    "(t p) h -> p t h", p=128)), "a1_kvl", 16)
        P.wait("t", "a1_kvl")
        if ridx == 0:
            P.wait("t", "a1q_ev", 64)
        for half in range(2):
            for qh4 in range(4):
                h = kvh * 4 + qh4
                qcol = h * TL + half * 512
                pav = c.pp[ridx % 2]
                pdn = c.pd[ridx % 2]
                P.wait("s", avs, P.cnt[avs])
                P.wait("t", nms, P.cnt[nms])

                def a1_av(i, last):
                    P.wait("t", exs, excnt[i])
                    fn = lambda E, i=i, pav=pav, last=last: E.matmul(
                        pav[:, :], vkv[:, i * 128:(i + 1) * 128],
                        pb[:, (i % 8) * 512:(i % 8) * 512 + 512],
                        start=(i == 0), stop=last)
                    if last:
                        P.opi("t", fn, avs, 1)
                    else:
                        P.op("t", fn)

                def a1_dnb(j, pdn=pdn):
                    if j == 0:
                        P.wait("t", "a1_nbc", P.cnt.get("a1_nbc", 0))
                    for u in range(4):
                        i = 4 * j + u
                        fn = lambda E, i=i, u=u, pdn=pdn, j=j: E.matmul(
                            pdn[32 * u:32 * u + 1, :], c.ones_b[:, :],
                            pb[:, (i % 8) * 512:(i % 8) * 512 + 512],
                            start=(j == 0), stop=(j == 15),
                            tile_position=(0, 32 * u))
                        if j == 15 and u == 3:
                            P.opi("t", fn, "a1_dnf", 1)
                        else:
                            P.op("t", fn)

                excnt = {}
                for g in range(32):
                    sshalf = (g % 2) * 1024
                    P.wait("t", exs, max(0, P.cnt[exs] - 1))
                    for u in range(2):
                        i = 2 * g + u
                        fn = lambda E, i=i, sshalf=sshalf, u=u, qcol=qcol: E.matmul(
                            c.ssb[:, sshalf + u * 512: sshalf + u * 512 + 512],
                            kkv[:, i * 128:(i + 1) * 128],
                            c.qb[:, qcol:qcol + 512], start=True, stop=True)
                        if u == 1:
                            P.opi("t", fn, "a1_smm", 1)
                        else:
                            P.op("t", fn)
                    if g == 2:
                        P.wait("s", "a1_dnf", P.cnt.get("a1_dnf", 0))
                    P.wait("s", "a1_smm")
                    ec = P.opi("s", lambda E, g=g, sshalf=sshalf: E.activation(
                        out=pb[:, (g % 4) * 1024:(g % 4) * 1024 + 1024],
                        in_=c.ssb[:, sshalf:sshalf + 1024],
                        func=AF.Exp, bias=c.consts[:, 1:2], scale=c.consts[:, 0:1]), exs, 1)
                    excnt[2 * g] = ec
                    excnt[2 * g + 1] = ec
                    if g == 1 and pend[0] is not None:
                        pend[0]()
                        pend[0] = None
                    if g >= 1:
                        a1_av(2 * (g - 1), False)
                        a1_av(2 * g - 1, False)
                        if g >= 2 and g % 2 == 0:
                            a1_dnb((g - 2) // 2)
                a1_av(62, False)
                a1_av(63, True)
                a1_dnb(15)
                avc = P.cnt[avs]
                dnfc = P.cnt["a1_dnf"]

                def mk_norm(pav=pav, pdn=pdn, qcol=qcol, avc=avc, dnfc=dnfc):
                    def plan():
                        P.wait("v", avs, avc)
                        P.wait("v", "a1_dnf", dnfc)
                        P.wait("v", nms, P.cnt[nms])
                        P.opi("v", lambda E: E.tensor_copy(
                            c.rBf[:, 0:512], pdn[:, 0:512]), "a1_dne", 1)
                        P.wait("t", "a1_dne")
                        P.opi("t", lambda E: E.matmul(
                            pdn[0:1, 0:512], sel4[:, :], c.rBf[:, 0:512],
                            start=True, stop=True), "a1_dsm", 1)
                        P.wait("v", "a1_dsm")
                        P.opi("v", lambda E: E.reciprocal(
                            out=c.rBf[0:1, 512:1024], in_=pdn[0:1, 0:512]), "a1_rec", 1)
                        P.wait("t", "a1_rec")
                        P.opi("t", lambda E: E.matmul(
                            pdn[:, :], c.ones_f[:, :], c.rBf[0:1, 512:1024],
                            start=True, stop=True), "a1_bcm", 1)
                        P.wait("v", "a1_bcm")
                        P.opi("v", lambda E: E.tensor_copy(
                            c.rBf[:, 0:512], pdn[:, :]), "a1_nbc", 1)
                        P.opi("v", lambda E: E.tensor_mul(
                            c.qb[:, qcol:qcol + 512], pav[:, :], c.rBf[:, 0:512]), nms, 1)
                    return plan

                pend[0] = mk_norm()
                ridx += 1
    if pend[0] is not None:
        pend[0]()
        pend[0] = None

    if CUT <= 6:
        return
    P.wait("t", nms)
    fused = getattr(c, "has_a1e", False)
    ocb = make_evac_residual_store(c, "a1o", io["x"], io["xo"],
                                   mir_sem="g:mir2" if fused else None,
                                   mir_gate=("a1q_mm", 64))
    emit_proj_fm(c, "a1o", io["woT"], 32, ocb, gate_g=(("a1q_mm", 64),),
                 in_sb=c.qb)
    P.wait("y", "a1o_st0", P.cnt["a1o_st0"])
    P.wait("y", "a1o_st1", P.cnt["a1o_st1"])


def build_l1a():
    raise NotImplementedError("standalone l1a removed; cache staging is fused "
                              "into the l0a phase (emit_a1_head/emit_a1_tail)")



def emit_l1f(c, io):
    nc, P = c.nc, c.P
    A = c.A
    c0 = A("f1_c0", [128, NCH * 96], F32).ap()
    m0 = A("f1_m0", [128, NCH * 96], mybir.dt.uint8).ap()
    c1 = A("f1_c1", [128, NCH * 24], F32).ap()
    m1 = A("f1_m1", [128, NCH * 24], mybir.dt.uint8).ap()
    wsum = A("f1_wsum", [128, NCH * 40], BF16).ap()
    wqt = A("f1_wqt", [40, D], BF16).ap()
    encb = A("f1_encb", [128, NCH * 40], BF16).ap()
    eb = A("f1_eb", [40, TL], BF16).ap()
    msc = A("f1_msc", [128, 96], F32).ap()
    biasb = A("f1_biasb", [128, 32], F32).ap()

    P.opi("y", lambda E: E.dma_start(out=biasb[:], in_=io["b_baseT"][:, :]), "f1_sld", 16)
    P.opi("y", lambda E: E.dma_start(
        out=c0.rearrange("p (k r) -> p k r", r=96),
        in_=io["c0r"].rearrange("(k p) r -> p k r", p=128)), "f1_sld", 16)
    P.opi("y", lambda E: E.dma_start(
        out=m0.rearrange("p (k r) -> p k r", r=96),
        in_=io["m0r"].rearrange("(k p) r -> p k r", p=128)), "f1_sld", 16)
    P.opi("y", lambda E: E.dma_start(
        out=c1.rearrange("p (k r) -> p k r", r=24),
        in_=io["c1r"].rearrange("(k p) r -> p k r", p=128)), "f1_sld", 16)
    P.opi("y", lambda E: E.dma_start(
        out=m1.rearrange("p (k r) -> p k r", r=24),
        in_=io["m1r"].rearrange("(k p) r -> p k r", p=128)), "f1_sld", 16)
    P.opi("g", lambda E: E.dma_start(
        out=encb.rearrange("p (k r) -> p k r", r=40),
        in_=io["encT"].rearrange("(k p) r -> p k r", p=128)), "f1_gld", 16)

    ln = emit_stats_ln(c, io["x"], "f1ln",
                       pre_sem="g:mir2" if getattr(c, "has_a1e", False) else None)

    P.wait("v", "f1_sld", 80)
    for oc in range(NCH):
        if oc > 0:
            P.wait("v", "f1_ws", 2 * oc)
        P.opi("v", lambda E, oc=oc: E.tensor_copy(
            msc[:, 0:96], m0[:, oc * 96:(oc + 1) * 96]), "f1_wa", 1)
        P.wait("v", "f1_wa")
        P.opi("v", lambda E, oc=oc: E.tensor_mul(
            msc[:, 0:96], msc[:, 0:96], c0[:, oc * 96:(oc + 1) * 96]), "f1_wb", 1)
        P.wait("v", "f1_wb")
        P.opi("v", lambda E: E.tensor_add(
            msc[:, 0:32], msc[:, 0:32], msc[:, 32:64]), "f1_wc", 1)
        P.wait("v", "f1_wc")
        P.opi("v", lambda E: E.tensor_add(
            msc[:, 0:32], msc[:, 0:32], msc[:, 64:96]), "f1_wd", 1)
        P.wait("v", "f1_wd")
        P.opi("v", lambda E, oc=oc: E.tensor_scalar(
            wsum[:, oc * 40:oc * 40 + 32], msc[:, 0:32], 0.5, None, ALU.mult),
            "f1_ws", 1)
        P.wait("v", "f1_ws", 2 * oc + 1)
        P.opi("v", lambda E, oc=oc: E.tensor_copy(
            msc[:, 0:24], m1[:, oc * 24:(oc + 1) * 24]), "f1_we", 1)
        P.wait("v", "f1_we")
        P.opi("v", lambda E, oc=oc: E.tensor_mul(
            msc[:, 0:24], msc[:, 0:24], c1[:, oc * 24:(oc + 1) * 24]), "f1_wf", 1)
        P.wait("v", "f1_wf")
        P.opi("v", lambda E: E.tensor_add(
            msc[:, 0:8], msc[:, 0:8], msc[:, 8:16]), "f1_wg", 1)
        P.wait("v", "f1_wg")
        P.opi("v", lambda E: E.tensor_add(
            msc[:, 0:8], msc[:, 0:8], msc[:, 16:24]), "f1_wh", 1)
        P.wait("v", "f1_wh")
        P.opi("v", lambda E, oc=oc: E.tensor_scalar(
            wsum[:, oc * 40 + 32:oc * 40 + 40], msc[:, 0:8], 0.5, None, ALU.mult),
            "f1_ws", 1)

    banks = [c.pp[0], c.pp[1], c.ss[0], c.ss[1], c.pd[0], c.pd[1], c.pt[0], c.pt[1]]
    P.wait("t", "f1ln_r3", 2)
    P.wait("t", "f1ln_b2", 4)
    gd = ["f1_qw0", "f1_qw1"]
    for oc in range(NCH):
        buf = oc % 2
        if oc >= 2:
            P.wait("y", "f1_qmm", (oc - 1) * 8)
        P.opi("y", lambda E, oc=oc, buf=buf: E.dma_start(
            out=c.wstrip[buf][:, 0:4096],
            in_=io["QT"][oc * 128:(oc + 1) * 128, :]), gd[buf], 16)
        P.wait("t", gd[buf])
        P.wait("t", "f1_ws", 2 * (oc + 1))
        for tw in range(8):
            fn = lambda E, oc=oc, buf=buf, tw=tw: E.matmul(
                banks[tw][0:40, :], wsum[:, oc * 40:(oc + 1) * 40],
                c.wstrip[buf][:, tw * 512:(tw + 1) * 512],
                start=(oc == 0), stop=(oc == NCH - 1))
            P.opi("t", fn, "f1_qmm", 1)
    P.wait("s", "f1_qmm", NCH * 8)
    for tw in range(8):
        P.opi("s", lambda E, tw=tw: E.copy(
            out=wqt[:, tw * 512:(tw + 1) * 512], in_=banks[tw][0:40, :]), "f1_qev", 1)

    P.wait("t", "f1_qev", 8)
    P.wait("t", ln)
    P.wait("t", "f1_gld", 16)
    for w in range(2):
        ps = c.pp[w]
        for dc in range(NCH):
            fn = lambda E, w=w, dc=dc, ps=ps: E.matmul(
                ps[0:40, :], encb[:, dc * 40:(dc + 1) * 40],
                c.xmir[:, dc * TL + w * 512: dc * TL + w * 512 + 512],
                start=(dc == 0), stop=(dc == NCH - 1))
            if dc == NCH - 1:
                P.opi("t", fn, "f1_emm", 1)
            else:
                P.op("t", fn)
        P.wait("s", "f1_emm")
        P.opi("s", lambda E, w=w, ps=ps: E.copy(
            out=eb[:, w * 512:(w + 1) * 512], in_=ps[0:40, :]), "f1_eev", 1)

    P.wait("t", "f1_eev", 2)

    def extra(E, oc, w, ps):
        return E.matmul(ps[:, :], wqt[:, oc * 128:(oc + 1) * 128],
                        eb[:, w * 512:(w + 1) * 512], start=False, stop=True)

    P.wait("s", "f1_sld", 80)
    ocb = make_evac_residual_store(c, "f1o", io["x"], io["xo"], bias_ap=biasb)
    emit_proj_fm(c, "f1o", io["wbT"], 32, ocb,
                 gate_g=(("f1_emm", 2), ("f1_qmm", NCH * 8)),
                 extra_chunk=extra)
    P.wait("y", "f1o_st0", P.cnt["f1o_st0"])
    P.wait("y", "f1o_st1", P.cnt["f1o_st1"])


def build_l1f():
    nc = bass.Bass("TRN2", target_bir_lowering=False)
    io = {}
    def din(name, shape, dt=F32):
        io[name] = nc.dram_tensor(name, shape, dt, kind="ExternalInput").ap()
    din("x", [D, TL])
    din("wbT", [32, 128, D]); din("b_baseT", [128, 32])
    din("c0r", [D, 96]); din("c1r", [D, 24])
    din("m0r", [D, 96], mybir.dt.uint8); din("m1r", [D, 24], mybir.dt.uint8)
    din("encT", [D, 40])
    din("QT", [D, D], BF16)
    io["xo"] = nc.dram_tensor("xo", [D, TL], F32, kind="ExternalOutput").ap()
    P = Prog(nc)
    c = alloc_common(nc, P)
    emit_l1f(c, io)
    P.finalize()
    return nc



ARENA_BYTES = 45056

PHASE_IN_SPECS = {
    "l0a": [("wqT", [32, 128, D], F32), ("wkT", [1, 128, D], F32), ("wvT", [1, 128, D], F32),
            ("woT", [32, 128, D], F32), ("wq_bT", [128, 32], F32), ("wo_bT", [128, 32], F32),
            ("wk_bT", [128, 1], F32), ("kv_brow", [1, 256], BF16),
            ("masks0", [32, 128, 256], BF16), ("sel4", [128, 1], F32)],
    "l0f": [("wbT", [32, 128, D], F32), ("wrT", [D, 8], F32), ("AT", [1, 128, D], F32),
            ("BfT", [128, D], F32), ("b_baseT", [128, 32], F32),
            ("Econst", [8, 128], BF16), ("ident", [128, 128], BF16)],
    "l1a": [("wqT", [32, 128, D], F32), ("wkT", [8, 128, D], F32), ("wvT", [8, 128, D], F32),
            ("woT", [32, 128, D], F32), ("kcache", [8192, 4096], F32),
            ("vcache", [8192, 4096], F32), ("blockidx", [2, 1], I32),
            ("selconst", [128, 2], I32), ("ropecs", [128, 2], F32),
            ("m1c", [128, 128], BF16), ("m2c", [128, 128], BF16),
            ("ident", [128, 128], BF16), ("sel4", [128, 1], F32)],
    "l1f": [("wbT", [32, 128, D], F32), ("b_baseT", [128, 32], F32), ("c0r", [D, 96], F32),
            ("c1r", [D, 24], F32), ("m0r", [D, 96], mybir.dt.uint8),
            ("m1r", [D, 24], mybir.dt.uint8), ("encT", [D, 40], F32),
            ("QT", [D, D], BF16)],
}

PHASE_INTERNALS = {
    "l0a": [("kcon_d", [128, TL], BF16), ("vcon_d", [TL, 128], BF16),
            ("kg_d", [512, TL], BF16), ("vg_d", [4096, 128], BF16)],
    "l0f": [],
    "l1a": [("kstage", [512, 1024], BF16), ("kcach_d", [1024, 512], BF16),
            ("vcach_d", [512, 1024], BF16), ("kgc_d", [8192, 512], BF16),
            ("vgc_d", [4096, 1024], BF16), ("knew_d", [1024, 1024], BF16),
            ("vnew_d", [1024, 1024], BF16), ("kgn_d", [8192, 1024], BF16),
            ("vgn_d", [8192, 1024], BF16)],
    "l1f": [],
}

PHASE_EMIT = {"l0a": emit_l0a, "l0f": emit_l0f, "l1a": emit_l1a, "l1f": emit_l1f}
PHASE_ST = {"l0a": "l0o", "l0f": "f0o", "l1a": "a1o", "l1f": "f1o"}


def build_fused():
    nc = bass.Bass("TRN2", target_bir_lowering=False)
    P = Prog(nc)
    c = alloc_common(nc, P)
    arena = nc.alloc_sbuf_tensor("arena", [128, ARENA_BYTES], mybir.dt.uint8)
    arena_base = nc.lookup_mloc("arena").addr

    xin = nc.dram_tensor("xin", [D, TL], F32, kind="ExternalInput").ap()
    xout = nc.dram_tensor("xout", [D, TL], F32, kind="ExternalOutput").ap()
    xi = [nc.dram_tensor(f"xi{i}", [D, TL], F32).ap() for i in range(3)]
    xs_chain = [xin, xi[0], xi[1], xi[2], xout]

    io_l1a = {}
    for tname, shape, dt in PHASE_IN_SPECS["l1a"]:
        io_l1a[tname] = nc.dram_tensor(f"l1a_{tname}", shape, dt,
                                       kind="ExternalInput").ap()
    for tname, shape, dt in PHASE_INTERNALS["l1a"]:
        io_l1a[tname] = nc.dram_tensor(f"l1a_{tname}", shape, dt).ap()
    c.has_a1e = True
    emit_a1_head(c, io_l1a)

    for pi, name in enumerate(("l0a", "l0f", "l1a", "l1f")):
        state = {"off": arena_base}

        def A(tname, shape, dtype, state=state, name=name):
            import functools
            import operator
            nbytes = functools.reduce(operator.mul, shape[1:], 1) * mybir.dt.size(dtype)
            nbytes = (nbytes + 31) & ~31
            h = nc.alloc_sbuf_tensor_at(f"{name}_{tname}", shape, dtype,
                                        offset=state["off"])
            state["off"] += nbytes
            assert state["off"] <= arena_base + ARENA_BYTES, \
                f"arena overflow in {name}: {state['off'] - arena_base}"
            return h
        c.A = A
        if name == "l1a":
            io = io_l1a
        else:
            io = {}
            for tname, shape, dt in PHASE_IN_SPECS[name]:
                io[tname] = nc.dram_tensor(f"{name}_{tname}", shape, dt,
                                           kind="ExternalInput").ap()
            for tname, shape, dt in PHASE_INTERNALS[name]:
                io[tname] = nc.dram_tensor(f"{name}_{tname}", shape, dt).ap()
        io["x"] = xs_chain[pi]
        io["xo"] = xs_chain[pi + 1]
        if pi > 0:
            prev = PHASE_ST[("l0a", "l0f", "l1a", "l1f")[pi - 1]]
            engs = ("g", "y") if name in ("l0f", "l1f") else ("t", "v", "s", "g", "y")
            for e in engs:
                P.wait(e, f"{prev}_st0", P.cnt[f"{prev}_st0"])
                P.wait(e, f"{prev}_st1", P.cnt[f"{prev}_st1"])
        if name == "l0f":
            P.wait("t", "g:a1kte", 32)
        P.new_scope()
        PHASE_EMIT[name](c, io)
        if name == "l0a":
            emit_a1_tail(c, io_l1a)
    P.finalize()
    return nc



def tile_strips(wT):
    n_oc = wT.shape[1] // 128
    a = wT.reshape(32, 128, n_oc, 128)
    return np.ascontiguousarray(a.transpose(2, 1, 0, 3).reshape(n_oc, 128, 4096))


def shard_x(x):
    out = []
    for core in range(8):
        b, cc = divmod(core, 4)
        rows = np.concatenate([x[b, t * 128:(t + 1) * 128, :] for t in OWNED[cc]], axis=0)
        out.append(np.ascontiguousarray(rows.T))
    return out


def unshard_x(shards):
    x = np.empty((2, 4096, 4096), dtype=np.float32)
    for core in range(8):
        b, cc = divmod(core, 4)
        rows = shards[core].T
        for li, t in enumerate(OWNED[cc]):
            x[b, t * 128:(t + 1) * 128, :] = rows[li * 128:(li + 1) * 128, :]
    return x


def build_l0_masks():
    per_cc = []
    for cc in range(4):
        m = np.zeros((32, 128, 256), dtype=np.float32)
        for w in range(4):
            g0, g1 = OWNED[cc][2 * w], OWNED[cc][2 * w + 1]
            for j in MASKED_W[w]:
                slot = w * 8 + MASK_SLOT[(w, j)]
                gk = GK[j]
                kidx = gk * 128 + np.arange(128)[:, None]
                m[slot, :, 0:128] = (kidx <= g0 * 128 + np.arange(128)[None, :])
                m[slot, :, 128:256] = (kidx <= g1 * 128 + np.arange(128)[None, :])
        per_cc.append(m.astype(ml_dtypes.bfloat16))
    return per_cc


def l0a_in_maps(inputs, xs):
    masks = build_l0_masks()
    f = np.float32
    wqT = tile_strips(np.asarray(inputs["L0_attn_wq_w"], f).T)
    wkT = tile_strips(np.asarray(inputs["L0_attn_wk_w"], f).T)
    wvT = tile_strips(np.asarray(inputs["L0_attn_wv_w"], f).T)
    woT = tile_strips(np.asarray(inputs["L0_attn_wo_w"], f).T)
    wq_bT = np.ascontiguousarray(np.asarray(inputs["L0_attn_wq_b"], f).reshape(32, 128).T)
    wo_bT = np.ascontiguousarray(np.asarray(inputs["L0_attn_wo_b"], f).reshape(32, 128).T)
    wk_bT = np.ascontiguousarray(np.asarray(inputs["L0_attn_wk_b"], f).reshape(128, 1))
    kv_brow = np.concatenate([
        np.asarray(inputs["L0_attn_wk_b"], f), np.asarray(inputs["L0_attn_wv_b"], f)
    ]).astype(ml_dtypes.bfloat16).reshape(1, 256)
    sel4 = np.zeros((128, 1), np.float32)
    sel4[[0, 32, 64, 96], 0] = 1.0
    maps = []
    for core in range(8):
        cc = core % 4
        maps.append({
            "x": xs[core], "wqT": wqT, "wkT": wkT, "wvT": wvT, "woT": woT,
            "wq_bT": wq_bT, "wo_bT": wo_bT, "wk_bT": wk_bT, "kv_brow": kv_brow,
            "masks0": masks[cc], "sel4": sel4,
        })
    return maps


def l0f_in_maps(inputs, xs):
    f = np.float32
    wbT = tile_strips(np.asarray(inputs["L0_ffn_W_base"], f).T)
    wrT = np.ascontiguousarray(np.asarray(inputs["L0_ffn_W_router"], f).T)
    A = np.asarray(inputs["L0_ffn_A"], f)
    B = np.asarray(inputs["L0_ffn_B"], f)
    AT = tile_strips(np.ascontiguousarray(A.reshape(128, 4096).T))
    BfT = np.ascontiguousarray(B.transpose(0, 2, 1).reshape(128, 4096))
    b_baseT = np.ascontiguousarray(np.asarray(inputs["L0_ffn_b_base"], f).reshape(32, 128).T)
    E = np.zeros((8, 128), np.float32)
    for e in range(8):
        E[e, e * 16:(e + 1) * 16] = 1.0
    Econst = E.astype(ml_dtypes.bfloat16)
    ident = np.eye(128, dtype=ml_dtypes.bfloat16)
    maps = []
    for core in range(8):
        maps.append({"x": xs[core], "wbT": wbT, "wrT": wrT, "AT": AT, "BfT": BfT,
                     "b_baseT": b_baseT, "Econst": Econst, "ident": ident})
    return maps


def l1a_in_maps(inputs, xs):
    f = np.float32
    wqT = tile_strips(np.asarray(inputs["L1_attn_wq"], f).T)
    wkT = tile_strips(np.asarray(inputs["L1_attn_wk"], f).T)
    wvT = tile_strips(np.asarray(inputs["L1_attn_wv"], f).T)
    woT = tile_strips(np.asarray(inputs["L1_attn_wo"], f).T)
    kcache = np.ascontiguousarray(np.asarray(inputs["L1_attn_k_cache"], f).reshape(8192, 4096))
    vcache = np.ascontiguousarray(np.asarray(inputs["L1_attn_v_cache"], f).reshape(8192, 4096))
    bi = np.sort(np.asarray(inputs["L1_attn_block_idx"]).astype(np.int32))
    cos = np.asarray(inputs["L1_attn_freqs_cos"], f)[0]
    sin = np.asarray(inputs["L1_attn_freqs_sin"], f)[0]
    ropecs = np.stack([np.repeat(cos, 2), np.repeat(sin, 2)], axis=1).astype(f)
    selconst = np.stack([np.arange(128) // 64, np.arange(128) % 64], axis=1).astype(np.int32)
    m1c = np.eye(128, dtype=ml_dtypes.bfloat16)
    m2 = np.zeros((128, 128), np.float32)
    for i in range(64):
        m2[2 * i, 2 * i + 1] = 1.0
        m2[2 * i + 1, 2 * i] = -1.0
    m2c = m2.astype(ml_dtypes.bfloat16)
    ident = np.eye(128, dtype=ml_dtypes.bfloat16)
    sel4 = np.zeros((128, 1), np.float32)
    sel4[[0, 32, 64, 96], 0] = 1.0
    maps = []
    for core in range(8):
        myblk = np.ascontiguousarray(bi[2 * core:2 * core + 2].reshape(2, 1))
        maps.append({"x": xs[core], "wqT": wqT, "wkT": wkT, "wvT": wvT, "woT": woT,
                     "kcache": kcache, "vcache": vcache, "blockidx": myblk,
                     "selconst": selconst, "ropecs": ropecs, "m1c": m1c,
                     "m2c": m2c, "ident": ident, "sel4": sel4})
    return maps


_QT_CACHE = {}


def fourier_qt():
    if "qt" not in _QT_CACHE:
        import math
        n = 4096
        t = np.arange(n)
        Q = np.zeros((n, n), np.float64)
        Q[:, 0] = 1.0 / math.sqrt(n)
        f_ = np.arange(1, n // 2)
        ang = 2.0 * np.pi * np.outer(t, f_) / n
        Q[:, 1:n - 1:2] = np.cos(ang) * math.sqrt(2.0 / n)
        Q[:, 2:n - 1:2] = np.sin(ang) * math.sqrt(2.0 / n)
        Q[:, n - 1] = np.where(t % 2 == 0, 1.0, -1.0) / math.sqrt(n)
        _QT_CACHE["qt"] = np.ascontiguousarray(Q.T).astype(ml_dtypes.bfloat16)
    return _QT_CACHE["qt"]


def l1f_in_maps(inputs, xs):
    f = np.float32
    wbT = tile_strips(np.asarray(inputs["L1_ffn_W_base"], f).T)
    b_baseT = np.ascontiguousarray(np.asarray(inputs["L1_ffn_b_base"], f).reshape(32, 128).T)
    c0r = np.ascontiguousarray(np.asarray(inputs["L1_ffn_coefs0"], f).transpose(1, 0, 2).reshape(4096, 96))
    c1r = np.ascontiguousarray(np.asarray(inputs["L1_ffn_coefs1"], f).transpose(1, 0, 2).reshape(4096, 24))
    m0r = np.ascontiguousarray(np.asarray(inputs["L1_ffn_mask0"]).transpose(1, 0, 2).reshape(4096, 96).astype(np.uint8))
    m1r = np.ascontiguousarray(np.asarray(inputs["L1_ffn_mask1"]).transpose(1, 0, 2).reshape(4096, 24).astype(np.uint8))
    encT = np.ascontiguousarray(np.concatenate([
        np.asarray(inputs["L1_ffn_enc0"], f),
        np.asarray(inputs["L1_ffn_enc1"], f)], axis=0).T)
    QT = fourier_qt()
    maps = []
    for core in range(8):
        maps.append({"x": xs[core], "wbT": wbT, "b_baseT": b_baseT,
                     "c0r": c0r, "c1r": c1r, "m0r": m0r, "m1r": m1r,
                     "encT": encT, "QT": QT})
    return maps


_NC_CACHE = {}


def _get_nc(name, builder):
    if name not in _NC_CACHE:
        _NC_CACHE[name] = builder()
    return _NC_CACHE[name]


PHASE_MAPFNS = {"l0a": None, "l0f": None, "l1a": None, "l1f": None}


def fused_in_maps(inputs, xs):
    per_phase = {n: PHASE_MAPFNS[n](inputs, xs) for n in PHASE_MAPFNS}
    merged = []
    for core in range(8):
        m = {"xin": xs[core]}
        for pname, pmaps in per_phase.items():
            for k, v in pmaps[core].items():
                if k == "x":
                    continue
                m[f"{pname}_{k}"] = v
        merged.append(m)
    return merged


PHASES = [
    ("l0a", build_l0a, l0a_in_maps),
    ("l0f", build_l0f, l0f_in_maps),
    ("l1a", build_l1a, l1a_in_maps),
    ("l1f", build_l1f, l1f_in_maps),
]

import os as _os
RUN_PHASES = _os.environ.get("KPHASES", "l0a,l0f,l1a,l1f").split(",")
KMODE = _os.environ.get("KMODE", "fused")

PHASE_MAPFNS.update({"l0a": l0a_in_maps, "l0f": l0f_in_maps,
                     "l1a": l1a_in_maps, "l1f": l1f_in_maps})


def kernel(**inputs):
    x = np.asarray(inputs["L0_attn_x"], dtype=np.float32)
    xs = shard_x(x)
    if KMODE == "fused":
        nc = _get_nc("fused", build_fused)
        res = run_bass_kernel_spmd(nc, fused_in_maps(inputs, xs),
                                   core_ids=list(range(8)))
        return unshard_x([np.ascontiguousarray(np.asarray(res.results[i]["xout"]))
                          for i in range(8)])
    for name, builder, mapfn in PHASES:
        if name not in RUN_PHASES:
            continue
        nc = _get_nc(name, builder)
        res = run_bass_kernel_spmd(nc, mapfn(inputs, xs), core_ids=list(range(8)))
        xs = [np.ascontiguousarray(np.asarray(res.results[i]["xo"])) for i in range(8)]
    return unshard_x(xs)
